# version 01
# baseline (speedup 1.0000x reference)
import contextlib
import numpy as np
import concourse.bass as bass
import concourse.mybir as mybir
from concourse.bass_utils import run_bass_kernel_spmd

F32 = mybir.dt.float32
BF16 = mybir.dt.bfloat16
AF = mybir.ActivationFunctionType
ALU = mybir.AluOpType
AX = mybir.AxisListType

S = 2048
D = 1024
NH = 16
E = 8
NT = S // 128
ND = D // 128
NQC = S // 512


class Plan:
    def __init__(self):
        self.steps = {"t": [], "s": [], "v": [], "g": [], "y": []}
        self.sem_val = {}

    def add(self, eng, fn):
        self.steps[eng].append(fn)

    def inc(self, sem_name, amount=1):
        v = self.sem_val.get(sem_name, 0) + amount
        self.sem_val[sem_name] = v
        return v


def build_nc():
    nc = bass.Bass("TRN2", target_bir_lowering=False)

    x_h = nc.dram_tensor("x", [S, D], F32, kind="ExternalInput").ap()
    wq_h = nc.dram_tensor("wq", [D, D], F32, kind="ExternalInput").ap()
    wk_h = nc.dram_tensor("wk", [D, D], F32, kind="ExternalInput").ap()
    wv_h = nc.dram_tensor("wv", [D, D], F32, kind="ExternalInput").ap()
    wo_h = nc.dram_tensor("wo", [D, D], F32, kind="ExternalInput").ap()
    wr_h = nc.dram_tensor("wr", [D, E], F32, kind="ExternalInput").ap()
    br_h = nc.dram_tensor("br", [1, E], F32, kind="ExternalInput").ap()
    w1_h = nc.dram_tensor("w1", [E, D, 4096], F32, kind="ExternalInput").ap()
    b1_h = nc.dram_tensor("b1", [E, 4096], F32, kind="ExternalInput").ap()
    w2_h = nc.dram_tensor("w2", [E, 4096, D], F32, kind="ExternalInput").ap()
    b2_h = nc.dram_tensor("b2", [E, D], F32, kind="ExternalInput").ap()
    id_h = nc.dram_tensor("ident", [128, 128], F32, kind="ExternalInput").ap()
    tril_h = nc.dram_tensor("tril", [128, 128], F32, kind="ExternalInput").ap()
    out_h = nc.dram_tensor("out", [S, D], F32, kind="ExternalOutput").ap()
    xnts_h = nc.dram_tensor("xnts", [128, ND * S], BF16, kind="Internal").ap()

    P = Plan()
    SEMS = ["dX0", "dX1", "dA", "dC", "dWq", "dWo", "dW1a", "dW1b",
            "dW2a", "dW2b", "dB0", "dB1", "dB2a", "dB2b", "dOut",
            "tT", "tS", "tV", "ph"]
    OWN = {"t": "tT", "s": "tS", "v": "tV"}

    with contextlib.ExitStack() as ctx:
        sem = {n: ctx.enter_context(nc.semaphore(n)) for n in SEMS}

        def W(eng_key, sname, val):
            if val > 0:
                P.add(eng_key, lambda e, s=sem[sname], v=val: e.wait_ge(s, v))

        def DR(eng_key):
            P.add(eng_key, lambda e: e.drain())

        def EMIT(eng_key, fn, inc_sem=None, inc_by=1):
            if inc_sem is None:
                P.add(eng_key, fn)
                return None
            v = P.inc(inc_sem, inc_by)
            P.add(eng_key, lambda e, f=fn, s=sem[inc_sem], a=inc_by: f(e).then_inc(s, a))
            return v

        bank_last = {}

        def bank_wait(eng_key, bank):
            la = bank_last.get(bank)
            if la is not None and la[0] != eng_key:
                W(eng_key, la[1], la[2])

        def bank_done(eng_key, bank, sname, val):
            bank_last[bank] = (eng_key, sname, val)

        gates = ctx.enter_context(nc.sbuf_tensor("gates", [128, NT * E], F32))
        id_f = ctx.enter_context(nc.sbuf_tensor("id_f", [128, 128], F32))
        id_b = ctx.enter_context(nc.sbuf_tensor("id_b", [128, 128], BF16))
        tril_b = ctx.enter_context(nc.sbuf_tensor("tril_b", [128, 128], BF16))
        br_t = ctx.enter_context(nc.sbuf_tensor("br_t", [128, E], F32))
        ones_b = ctx.enter_context(nc.sbuf_tensor("ones_b", [1, 128], BF16))
        eps_t = ctx.enter_context(nc.sbuf_tensor("eps_t", [128, 1], F32))
        wr_sb = ctx.enter_context(nc.sbuf_tensor("wr_sb", [128, ND * E], BF16))
        qkv_scope = ctx.enter_context(contextlib.ExitStack())
        qT = qkv_scope.enter_context(nc.sbuf_tensor("qT", [128, ND * S], BF16))
        kT = qkv_scope.enter_context(nc.sbuf_tensor("kT", [128, ND * S], BF16))
        v_sb = qkv_scope.enter_context(nc.sbuf_tensor("v_sb", [128, NT * NH * 65], BF16))
        acc_scope = ctx.enter_context(contextlib.ExitStack())
        acc = None

        block = ctx.enter_context(nc.Block())

        EMIT("y", lambda e: e.dma_start(out=id_f[:, :], in_=id_h), "dA", 16)
        EMIT("g", lambda e: e.dma_start(out=id_b[:, :], in_=id_h), "dC", 16)
        EMIT("g", lambda e: e.dma_start(out=tril_b[:, :], in_=tril_h), "dC", 16)
        EMIT("g", lambda e: e.dma_start(out=br_t[:, :], in_=br_h.partition_broadcast(128)), "dC", 16)
        EMIT("g", lambda e: e.dma_start(
            out=wr_sb[:, :].rearrange("p (t e) -> p t e", t=ND),
            in_=wr_h.rearrange("(t p) e -> p t e", p=128)), "dC", 16)
        vC = P.sem_val["dC"]
        EMIT("v", lambda e: e.memset(ones_b[:, :], 1.0), "tV")
        veps = EMIT("v", lambda e: e.memset(eps_t[:, :], 1e-5), "tV")
        EMIT("v", lambda e: e.memset(
            v_sb[:, :].rearrange("p (g c) -> p g c", c=65)[:, :, 64:65], 1.0), "tV")

        with contextlib.ExitStack() as a1:
            xnT = a1.enter_context(nc.sbuf_tensor("xnT", [128, ND * S], BF16))
            x_sb = a1.enter_context(nc.sbuf_tensor("x_sb", [128, 2 * D], F32))
            xc = a1.enter_context(nc.sbuf_tensor("xc", [128, D], F32))
            xnb = a1.enter_context(nc.sbuf_tensor("xnb", [128, D], BF16))
            sq = a1.enter_context(nc.sbuf_tensor("sq", [128, D], F32))
            st1 = a1.enter_context(nc.sbuf_tensor("st1", [128, 8], F32))
            rt8 = a1.enter_context(nc.sbuf_tensor("rt8", [128, 6 * E], F32))
            rs1 = a1.enter_context(nc.sbuf_tensor("rs1", [128, 8], F32))
            w_sb = a1.enter_context(nc.sbuf_tensor("w_sb", [128, ND * D], BF16))

            pbT = a1.enter_context(nc.psum_tensor("pbT", [128, 1024], BF16))
            pb0 = a1.enter_context(nc.psum_tensor("pb0", [128, 512], F32))
            pb1 = a1.enter_context(nc.psum_tensor("pb1", [128, 512], F32))
            pb2 = a1.enter_context(nc.psum_tensor("pb2", [128, 512], F32))

            EMIT("g", lambda e: e.dma_start(
                out=w_sb[:, :].rearrange("p (t f) -> p t f", t=ND),
                in_=wq_h.rearrange("(t p) f -> p t f", p=128)), "dWq", 16)

            x_slot_free = [None, None]
            last_xn_evac = None
            last_tr = None

            for j in range(NT):
                slot = j % 2
                dxs = "dX0" if slot == 0 else "dX1"
                if x_slot_free[slot] is not None:
                    W("y", x_slot_free[slot][0], x_slot_free[slot][1])
                vload = EMIT("y", lambda e, j=j, slot=slot: e.dma_start(
                    out=x_sb[:, slot * D:(slot + 1) * D],
                    in_=x_h.rearrange("(j p) d -> j p d", p=128)[j]), dxs, 16)
                W("v", dxs, vload)
                EMIT("v", lambda e, slot=slot: e.reduce_sum(
                    st1[:, 0:1], x_sb[:, slot * D:(slot + 1) * D], axis=AX.X, negate=True))
                DR("v")
                EMIT("v", lambda e: e.tensor_scalar_mul(st1[:, 0:1], st1[:, 0:1], 1.0 / D))
                DR("v")
                vxc = EMIT("v", lambda e, slot=slot: e.tensor_scalar_add(
                    xc[:, :], x_sb[:, slot * D:(slot + 1) * D], st1[:, 0:1]), "tV")
                x_slot_free[slot] = ("tV", vxc)
                DR("v")
                EMIT("v", lambda e: e.tensor_mul(sq[:, :], xc[:, :], xc[:, :]))
                DR("v")
                vvar = EMIT("v", lambda e: e.reduce_sum(st1[:, 1:2], sq[:, :], axis=AX.X), "tV")
                W("s", "tV", vvar)
                EMIT("s", lambda e: e.activation(st1[:, 2:3], st1[:, 1:2], AF.Ln,
                                                 bias=eps_t[:, 0:1], scale=1.0 / D))
                DR("s")
                vrstd = EMIT("s", lambda e: e.activation(st1[:, 3:4], st1[:, 2:3], AF.Exp,
                                                         scale=-0.5), "tS")
                W("v", "tS", vrstd)
                if last_tr is not None:
                    W("v", "tT", last_tr)
                vxn = EMIT("v", lambda e: e.tensor_scalar_mul(xnb[:, :], xc[:, :], st1[:, 3:4]), "tV")
                W("t", "tV", vxn)
                W("t", "dC", vC)
                bank_wait("t", "pbT")
                def _tr(e, j=j):
                    ins = None
                    for d in range(ND):
                        ins = e.transpose(pbT[:, d * 128:(d + 1) * 128],
                                          xnb[:, d * 128:(d + 1) * 128], id_b[:, :])
                    return ins
                vtr = EMIT("t", _tr, "tT")
                last_tr = vtr
                bank_done("t", "pbT", "tT", vtr)
                bank_wait("v", "pbT")
                def _ev(e, j=j):
                    ins = None
                    for d in range(ND):
                        ins = e.tensor_copy(xnT[:, d * S + 128 * j: d * S + 128 * (j + 1)],
                                            pbT[:, d * 128:(d + 1) * 128])
                    return ins
                vev = EMIT("v", _ev, "tV")
                bank_done("v", "pbT", "tV", vev)
                last_xn_evac = vev

                W("t", "tV", vev)
                bank_wait("t", "pb2")
                def _rt(e, j=j):
                    ins = None
                    for d in range(ND):
                        ins = e.matmul(pb2[:, 0:E],
                                       xnT[:, d * S + 128 * j: d * S + 128 * (j + 1)],
                                       wr_sb[:, d * E:(d + 1) * E],
                                       start=(d == 0), stop=(d == ND - 1))
                    return ins
                vrt = EMIT("t", _rt, "tT")
                bank_done("t", "pb2", "tT", vrt)
                bank_wait("v", "pb2")
                lg = rt8[:, 0 * E:1 * E]
                eg = rt8[:, 1 * E:2 * E]
                ind = rt8[:, 2 * E:3 * E]
                ind2 = rt8[:, 3 * E:4 * E]
                u = rt8[:, 4 * E:5 * E]
                p_ = rt8[:, 5 * E:6 * E]
                vpb2 = EMIT("v", lambda e, lg=lg: e.tensor_add(lg, pb2[:, 0:E], br_t[:, :]), "tV")
                bank_done("v", "pb2", "tV", vpb2)
                DR("v")
                EMIT("v", lambda e, lg=lg: e.reduce_max(rs1[:, 0:1], lg, axis=AX.X, negate=True))
                DR("v")
                EMIT("v", lambda e, lg=lg, ind=ind: e.tensor_scalar(
                    ind, lg, rs1[:, 0:1], 0.0, op0=ALU.add, op1=ALU.is_ge))
                DR("v")
                EMIT("v", lambda e, lg=lg, ind=ind, u=u: e.scalar_tensor_tensor(
                    u, ind, -1e30, lg, op0=ALU.mult, op1=ALU.add))
                DR("v")
                vnm2 = EMIT("v", lambda e, u=u: e.reduce_max(
                    rs1[:, 1:2], u, axis=AX.X, negate=True), "tV")
                W("s", "tV", vnm2)
                DR("s")
                vexp = EMIT("s", lambda e, lg=lg, eg=eg: e.activation(
                    eg, lg, AF.Exp, bias=rs1[:, 0:1], scale=1.0), "tS")
                W("v", "tS", vexp)
                EMIT("v", lambda e, lg=lg, ind2=ind2: e.tensor_scalar(
                    ind2, lg, rs1[:, 1:2], 0.0, op0=ALU.add, op1=ALU.is_ge))
                DR("v")
                EMIT("v", lambda e, eg=eg, ind2=ind2, p_=p_: e.tensor_mul(p_, eg, ind2))
                DR("v")
                EMIT("v", lambda e, p_=p_: e.reduce_sum(rs1[:, 2:3], p_, axis=AX.X))
                DR("v")
                EMIT("v", lambda e: e.reciprocal(rs1[:, 3:4], rs1[:, 2:3]))
                DR("v")
                EMIT("v", lambda e, j=j, p_=p_: e.tensor_scalar_mul(
                    gates[:, j * E:(j + 1) * E], p_, rs1[:, 3:4]), "tV")
                DR("v")

            def proj_qk(dst, dsem_val):
                for u in range(ND):
                    for scq in range(NQC):
                        par = (u * NQC + scq) % 2
                        pb = pb0 if par == 0 else pb1
                        pbn = "pb0" if par == 0 else "pb1"
                        W("t", "dWq", dsem_val)
                        if u == 0 and scq == 0:
                            W("t", "tV", last_xn_evac)
                        bank_wait("t", pbn)
                        def _mm(e, u=u, scq=scq, pb=pb):
                            ins = None
                            for d in range(ND):
                                ins = e.matmul(pb[:, :],
                                               w_sb[:, d * D + 128 * u: d * D + 128 * (u + 1)],
                                               xnT[:, d * S + 512 * scq: d * S + 512 * (scq + 1)],
                                               start=(d == 0), stop=(d == ND - 1))
                            return ins
                        vmm = EMIT("t", _mm, "tT")
                        bank_done("t", pbn, "tT", vmm)
                        ekey = "s" if par == 0 else "v"
                        bank_wait(ekey, pbn)
                        if ekey == "s":
                            vev = EMIT("s", lambda e, u=u, scq=scq, pb=pb, dst=dst: e.activation(
                                dst[:, u * S + 512 * scq: u * S + 512 * (scq + 1)],
                                pb[:, :], AF.Copy), "tS")
                        else:
                            vev = EMIT("v", lambda e, u=u, scq=scq, pb=pb, dst=dst: e.tensor_copy(
                                dst[:, u * S + 512 * scq: u * S + 512 * (scq + 1)],
                                pb[:, :]), "tV")
                        bank_done(ekey, pbn, OWN[ekey], vev)

            proj_qk(qT, 16)
            W("g", "tT", P.sem_val["tT"])
            vwk = EMIT("g", lambda e: e.dma_start(
                out=w_sb[:, :].rearrange("p (t f) -> p t f", t=ND),
                in_=wk_h.rearrange("(t p) f -> p t f", p=128)), "dWq", 16)
            proj_qk(kT, vwk)
            W("g", "tT", P.sem_val["tT"])
            vwv = EMIT("g", lambda e: e.dma_start(
                out=w_sb[:, :].rearrange("p (t f) -> p t f", t=ND),
                in_=wv_h.rearrange("(t p) f -> p t f", p=128)), "dWq", 16)
            for j in range(NT):
                for ch in range(2):
                    par = (j * 2 + ch) % 2
                    pb = pb0 if par == 0 else pb1
                    pbn = "pb0" if par == 0 else "pb1"
                    W("t", "dWq", vwv)
                    bank_wait("t", pbn)
                    def _mmv(e, j=j, ch=ch, pb=pb):
                        ins = None
                        for d in range(ND):
                            ins = e.matmul(pb[:, :],
                                           xnT[:, d * S + 128 * j: d * S + 128 * (j + 1)],
                                           w_sb[:, d * D + 512 * ch: d * D + 512 * (ch + 1)],
                                           start=(d == 0), stop=(d == ND - 1))
                        return ins
                    vmm = EMIT("t", _mmv, "tT")
                    bank_done("t", pbn, "tT", vmm)
                    ekey = "s" if ch == 0 else "v"
                    bank_wait(ekey, pbn)
                    base = j * NH * 65 + ch * 8 * 65
                    if ekey == "s":
                        vev = EMIT("s", lambda e, base=base, pb=pb: e.activation(
                            v_sb[:, base:base + 8 * 65].rearrange("p (h c) -> p h c", c=65)[:, :, 0:64],
                            pb[:, :].rearrange("p (h c) -> p h c", c=64), AF.Copy), "tS")
                    else:
                        vev = EMIT("v", lambda e, base=base, pb=pb: e.tensor_copy(
                            v_sb[:, base:base + 8 * 65].rearrange("p (h c) -> p h c", c=65)[:, :, 0:64],
                            pb[:, :].rearrange("p (h c) -> p h c", c=64)), "tV")
                    bank_done(ekey, pbn, OWN[ekey], vev)

            W("y", "tV", P.sem_val["tV"])
            W("y", "tS", P.sem_val["tS"])
            EMIT("y", lambda e: e.dma_start(out=xnts_h, in_=xnT[:, :]), "dA", 16)

            for ek in ["t", "s", "v"]:
                W(ek, OWN[ek], P.sem_val[OWN[ek]])
                P.add(ek, lambda e: e.nop().then_inc(sem["ph"], 1))
                P.inc("ph")
            W("y", "dA", P.sem_val["dA"])
            W("y", "dX0", P.sem_val["dX0"])
            W("y", "dX1", P.sem_val["dX1"])
            P.add("y", lambda e: e.nop().then_inc(sem["ph"], 1))
            P.inc("ph")
            W("g", "dWq", P.sem_val["dWq"])
            W("g", "dC", P.sem_val["dC"])
            P.add("g", lambda e: e.nop().then_inc(sem["ph"], 1))
            bar_a1 = P.inc("ph")
        for ek in ["t", "s", "v", "g", "y"]:
            W(ek, "ph", bar_a1)
        bank_last.clear()
        acc = acc_scope.enter_context(nc.sbuf_tensor("acc", [128, NT * D], F32, side="right"))

        with contextlib.ExitStack() as a2:
            pT = a2.enter_context(nc.sbuf_tensor("pT", [128, 16 * 512], BF16))
            ctx_all = a2.enter_context(nc.sbuf_tensor("ctx_all", [128, 8 * 512], BF16))
            wo_sb = a2.enter_context(nc.sbuf_tensor("wo_sb", [128, 8 * 1024], BF16))
            rrow_f = a2.enter_context(nc.sbuf_tensor("rrow_f", [1, 512], F32))
            rrow_b = a2.enter_context(nc.sbuf_tensor("rrow_b", [1, 512], BF16))
            bc_sb = a2.enter_context(nc.sbuf_tensor("bc_sb", [64, 512], BF16))

            pL = [a2.enter_context(nc.psum_tensor("pL0", [128, 512], F32)),
                  a2.enter_context(nc.psum_tensor("pL1", [128, 512], F32))]
            pC = [a2.enter_context(nc.psum_tensor("pC0", [65, 512], F32)),
                  a2.enter_context(nc.psum_tensor("pC1", [65, 512], F32))]
            pB = a2.enter_context(nc.psum_tensor("pB", [64, 512], F32))
            pO = [a2.enter_context(nc.psum_tensor("pO0", [128, 512], F32)),
                  a2.enter_context(nc.psum_tensor("pO1", [128, 512], F32))]

            vwo = EMIT("g", lambda e: e.dma_start(
                out=wo_sb[:, :].rearrange("p (r d) -> p r d", r=8),
                in_=wo_h.rearrange("(r p) d -> p r d", p=128)), "dWo", 16)

            lctr = 0
            pv_reads = {}
            wo_waited = False
            for qc in range(NQC):
                ctx_ready = []
                for h in range(NH):
                    hp = h % 2
                    hb = h // 2
                    cpn = "pC%d" % hp
                    tmax = 4 * qc + 3
                    for t in range(tmax + 1):
                        qv0 = max(0, 128 * t - 512 * qc)
                        lb = lctr % 2
                        lctr += 1
                        lbn = "pL%d" % lb
                        bank_wait("t", lbn)
                        vlg = EMIT("t", lambda e, t=t, qc=qc, hp=hp, hb=hb, qv0=qv0, lb=lb:
                                   e.matmul(pL[lb][:, qv0:512],
                                            kT[64 * hp:64 * hp + 64,
                                               hb * S + 128 * t: hb * S + 128 * (t + 1)],
                                            qT[64 * hp:64 * hp + 64,
                                               hb * S + 512 * qc + qv0: hb * S + 512 * (qc + 1)],
                                            start=True, stop=True), "tT")
                        bank_done("t", lbn, "tT", vlg)
                        bank_wait("s", lbn)
                        if t in pv_reads:
                            W("s", "tT", pv_reads[t])
                        vex = EMIT("s", lambda e, t=t, qv0=qv0, lb=lb: e.activation(
                            pT[:, 512 * t + qv0: 512 * (t + 1)],
                            pL[lb][:, qv0:512], AF.Exp, scale=0.125), "tS")
                        bank_done("s", lbn, "tS", vex)
                        gate = ("tS", vex)
                        if t >= 4 * qc:
                            W("v", "tS", vex)
                            vmask = EMIT("v", lambda e, t=t, qv0=qv0: e.tensor_mul(
                                pT[:, 512 * t + qv0: 512 * t + qv0 + 128],
                                pT[:, 512 * t + qv0: 512 * t + qv0 + 128],
                                tril_b[:, :]), "tV")
                            DR("v")
                            gate = ("tV", vmask)
                        W("t", gate[0], gate[1])
                        if t == 0:
                            bank_wait("t", cpn)
                        vpv = EMIT("t", lambda e, t=t, h=h, hp=hp, qv0=qv0,
                                   first=(t == 0), last=(t == tmax):
                                   e.matmul(pC[hp][:, qv0:512],
                                            v_sb[:, t * NH * 65 + 65 * h: t * NH * 65 + 65 * h + 65],
                                            pT[:, 512 * t + qv0: 512 * (t + 1)],
                                            start=first, stop=last), "tT")
                        pv_reads[t] = vpv
                    bank_done("t", cpn, "tT", vpv)
                    bank_wait("v", cpn)
                    DR("v")
                    EMIT("v", lambda e, hp=hp: e.reciprocal(rrow_f[:, :], pC[hp][64:65, :]))
                    DR("v")
                    vrb = EMIT("v", lambda e: e.tensor_copy(rrow_b[:, :], rrow_f[:, :]), "tV")
                    W("t", "tV", vrb)
                    bank_wait("t", "pB")
                    vbc = EMIT("t", lambda e: e.matmul(pB[0:64, :], ones_b[:, 0:64], rrow_b[:, :],
                                                       start=True, stop=True), "tT")
                    bank_done("t", "pB", "tT", vbc)
                    bank_wait("v", "pB")
                    EMIT("v", lambda e: e.tensor_copy(bc_sb[:, :], pB[0:64, :]))
                    DR("v")
                    vnz = EMIT("v", lambda e, hp=hp, hb=hb: e.tensor_mul(
                        ctx_all[64 * hp:64 * hp + 64, 512 * hb: 512 * (hb + 1)],
                        pC[hp][0:64, :], bc_sb[:, :]), "tV")
                    DR("v")
                    bank_done("v", "pB", "tV", vnz)
                    bank_done("v", cpn, "tV", vnz)
                    ctx_ready.append(vnz)
                if not wo_waited:
                    W("t", "dWo", vwo)
                    wo_waited = True
                for j4 in range(4):
                    for half in range(2):
                        bank_wait("t", "pO0")
                        bank_wait("t", "pO1")
                        W("t", "tV", ctx_ready[-1])
                        def _op(e, j4=j4, half=half):
                            ins = None
                            for h in range(NH):
                                hp = h % 2
                                hb = h // 2
                                ins = e.matmul(
                                    pO[hp][:, :],
                                    ctx_all[64 * hp:64 * hp + 64,
                                            512 * hb + 128 * j4: 512 * hb + 128 * (j4 + 1)],
                                    wo_sb[64 * hp:64 * hp + 64,
                                          1024 * hb + 512 * half: 1024 * hb + 512 * (half + 1)],
                                    start=(h < 2), stop=(h >= NH - 2))
                            return ins
                        vop = EMIT("t", _op, "tT")
                        bank_done("t", "pO0", "tT", vop)
                        bank_done("t", "pO1", "tT", vop)
                        bank_wait("v", "pO0")
                        bank_wait("v", "pO1")
                        jg = 4 * qc + j4
                        EMIT("v", lambda e, jg=jg, half=half: e.tensor_copy(
                            acc[:, jg * D + 512 * half: jg * D + 512 * (half + 1)], pO[0][:, :]))
                        DR("v")
                        voe = EMIT("v", lambda e, jg=jg, half=half: e.tensor_add(
                            acc[:, jg * D + 512 * half: jg * D + 512 * (half + 1)],
                            acc[:, jg * D + 512 * half: jg * D + 512 * (half + 1)],
                            pO[1][:, :]), "tV")
                        DR("v")
                        bank_done("v", "pO0", "tV", voe)
                        bank_done("v", "pO1", "tV", voe)

            for ek in ["t", "s", "v"]:
                W(ek, OWN[ek], P.sem_val[OWN[ek]])
                P.add(ek, lambda e: e.nop().then_inc(sem["ph"], 1))
                P.inc("ph")
            W("y", "dA", P.sem_val["dA"])
            P.add("y", lambda e: e.nop().then_inc(sem["ph"], 1))
            P.inc("ph")
            W("g", "dWo", P.sem_val["dWo"])
            P.add("g", lambda e: e.nop().then_inc(sem["ph"], 1))
            bar_a2 = P.inc("ph")
        for ek in ["t", "s", "v", "g", "y"]:
            W(ek, "ph", bar_a2)
        bank_last.clear()
        qkv_scope.close()

        with contextlib.ExitStack() as f:
            xnT2 = f.enter_context(nc.sbuf_tensor("xnT2", [128, ND * S], BF16))
            hT = f.enter_context(nc.sbuf_tensor("hT", [128, 8 * S], BF16))
            w2q = f.enter_context(nc.sbuf_tensor("w2q", [128, 2 * 8 * D], BF16))
            w1r = f.enter_context(nc.sbuf_tensor("w1r", [128, 2 * ND * 512], BF16))
            b1st = f.enter_context(nc.sbuf_tensor("b1st", [32, 2 * 128], F32))
            b1sb = f.enter_context(nc.sbuf_tensor("b1sb", [128, 2 * 32], F32))
            b2r = f.enter_context(nc.sbuf_tensor("b2r", [1, 2 * D], BF16))

            pH = [f.enter_context(nc.psum_tensor("pH%d" % i, [128, 512], F32)) for i in range(4)]
            pY = [f.enter_context(nc.psum_tensor("pY%d" % i, [128, 512], F32)) for i in range(4)]

            vxl = EMIT("y", lambda e: e.dma_start(out=xnT2[:, :], in_=xnts_h), "dA", 16)
            W("t", "dA", vxl)
            DR("v")

            w1_free = [None, None]
            w2_free = [None, None]
            b1_free = [None, None]
            b2_free = [None, None]
            w1_loads = [0, 0]
            w2_loads = [0, 0]
            b1_loads = [0, 0]
            b2_loads = [0, 0]
            w1_idx = 0

            for e_i in range(E):
                es = e_i % 2
                if b1_free[es] is not None:
                    W("y", b1_free[es][0], b1_free[es][1])
                b1_loads[es] += 1
                EMIT("y", lambda e, e_i=e_i, es=es: e.dma_start(
                    out=b1st[:, es * 128:(es + 1) * 128],
                    in_=b1_h[e_i].rearrange("(a b) -> a b", b=128)), "dB%d" % es, 16)
                if b2_free[es] is not None:
                    W("g", b2_free[es][0], b2_free[es][1])
                b2_loads[es] += 1
                EMIT("g", lambda e, e_i=e_i, es=es: e.dma_start(
                    out=b2r[:, es * D:(es + 1) * D],
                    in_=b2_h[e_i:e_i + 1, :]), "dB2%s" % ("a" if es == 0 else "b"), 16)
                W("t", "dB%d" % es, 16 * b1_loads[es])
                bank_wait("t", "pH0")
                vtr = EMIT("t", lambda e, es=es: e.transpose(
                    pH[0][:, 0:32], b1st[:, es * 128:(es + 1) * 128],
                    id_f[0:32, 0:32]), "tT")
                bank_done("t", "pH0", "tT", vtr)
                b1_free[es] = ("tT", vtr)
                bank_wait("v", "pH0")
                DR("v")
                vb1e = EMIT("v", lambda e, es=es: e.tensor_copy(
                    b1sb[:, es * 32:(es + 1) * 32], pH[0][:, 0:32]), "tV")
                DR("v")
                bank_done("v", "pH0", "tV", vb1e)
                b1_ready = vb1e

                for q in range(4):
                    ws = (e_i * 4 + q) % 2
                    if w2_free[ws] is not None:
                        W("g", w2_free[ws][0], w2_free[ws][1])
                    w2_loads[ws] += 1
                    EMIT("g", lambda e, e_i=e_i, q=q, ws=ws: e.dma_start(
                        out=w2q[:, ws * 8 * D:(ws + 1) * 8 * D].rearrange("p (t d) -> p t d", t=8),
                        in_=w2_h[e_i].rearrange("(qq t p) d -> qq p t d", qq=4, p=128)[q]),
                        "dW2%s" % ("a" if ws == 0 else "b"), 16)
                    w1v = []
                    for hh in range(2):
                        s_ = w1_idx % 2
                        w1_idx += 1
                        if w1_free[s_] is not None:
                            W("g", w1_free[s_][0], w1_free[s_][1])
                        w1_loads[s_] += 1
                        EMIT("g", lambda e, e_i=e_i, q=q, hh=hh, s_=s_: e.dma_start(
                            out=w1r[:, s_ * ND * 512:(s_ + 1) * ND * 512].rearrange(
                                "p (t f) -> p t f", t=ND),
                            in_=w1_h[e_i].rearrange("(t p) f -> p t f", p=128)[
                                :, :, 1024 * q + 512 * hh: 1024 * q + 512 * (hh + 1)]),
                            "dW1%s" % ("a" if s_ == 0 else "b"), 16)
                        w1v.append((s_, 16 * w1_loads[s_]))

                    hT_evacs = []
                    for ft in range(8):
                        hh = ft // 4
                        s_, wv_ = w1v[hh]
                        fo = (ft % 4) * 128
                        W("t", "dW1%s" % ("a" if s_ == 0 else "b"), wv_)
                        for sc in range(NQC):
                            bank_wait("t", "pH%d" % sc)
                        def _w1(e, s_=s_, fo=fo):
                            ins = None
                            for d in range(ND):
                                for sc in range(NQC):
                                    ins = e.matmul(
                                        pH[sc][:, :],
                                        w1r[:, s_ * ND * 512 + d * 512 + fo:
                                            s_ * ND * 512 + d * 512 + fo + 128],
                                        xnT2[:, d * S + 512 * sc: d * S + 512 * (sc + 1)],
                                        start=(d == 0), stop=(d == ND - 1))
                            return ins
                        vw1m = EMIT("t", _w1, "tT")
                        for sc in range(NQC):
                            bank_done("t", "pH%d" % sc, "tT", vw1m)
                        if ft % 4 == 3:
                            w1_free[s_] = ("tT", vw1m)
                        for sc in range(NQC):
                            ekey = "s" if sc < 2 else "v"
                            bank_wait(ekey, "pH%d" % sc)
                            if ekey == "s":
                                W("s", "tV", b1_ready)
                                vev = EMIT("s", lambda e, sc=sc, ft=ft, es=es, q=q: e.activation(
                                    hT[:, ft * S + 512 * sc: ft * S + 512 * (sc + 1)],
                                    pH[sc][:, :], AF.Relu,
                                    bias=b1sb[:, es * 32 + 8 * q + ft:
                                              es * 32 + 8 * q + ft + 1]), "tS")
                            else:
                                vev = EMIT("v", lambda e, sc=sc, ft=ft, es=es, q=q: e.tensor_scalar(
                                    hT[:, ft * S + 512 * sc: ft * S + 512 * (sc + 1)],
                                    pH[sc][:, :],
                                    b1sb[:, es * 32 + 8 * q + ft: es * 32 + 8 * q + ft + 1], 0.0,
                                    op0=ALU.add, op1=ALU.max), "tV")
                            bank_done(ekey, "pH%d" % sc, OWN[ekey], vev)
                            hT_evacs.append((ekey, vev))

                    W("t", "dW2%s" % ("a" if ws == 0 else "b"), 16 * w2_loads[ws])
                    if q == 0:
                        W("t", "dB2%s" % ("a" if es == 0 else "b"), 16 * b2_loads[es])
                    DR("v")
                    for j in range(NT):
                        pp = (j % 2) * 2
                        for dh in range(2):
                            bank_wait("t", "pY%d" % (pp + dh))
                        if j == 0:
                            for (ekey, vev) in hT_evacs:
                                W("t", OWN[ekey], vev)
                        def _w2(e, ws=ws, es=es, q=q, j=j, pp=pp):
                            ins = None
                            for ft in range(8):
                                for dh in range(2):
                                    ins = e.matmul(
                                        pY[pp + dh][:, :],
                                        hT[:, ft * S + 128 * j: ft * S + 128 * (j + 1)],
                                        w2q[:, ws * 8 * D + ft * D + 512 * dh:
                                            ws * 8 * D + ft * D + 512 * (dh + 1)],
                                        start=(ft == 0), stop=(ft == 7 and q != 0))
                            if q == 0:
                                for dh in range(2):
                                    ins = e.matmul(
                                        pY[pp + dh][:, :],
                                        ones_b[:, 0:128],
                                        b2r[:, es * D + 512 * dh: es * D + 512 * (dh + 1)],
                                        start=False, stop=True)
                            return ins
                        vw2m = EMIT("t", _w2, "tT")
                        for dh in range(2):
                            bank_done("t", "pY%d" % (pp + dh), "tT", vw2m)
                        if j == NT - 1:
                            w2_free[ws] = ("tT", vw2m)
                            if q == 0:
                                b2_free[es] = ("tT", vw2m)
                        for dh in range(2):
                            bank_wait("v", "pY%d" % (pp + dh))
                        def _ga(e, e_i=e_i, j=j, pp=pp):
                            ins = None
                            for dh in range(2):
                                ins = e.scalar_tensor_tensor(
                                    acc[:, j * D + 512 * dh: j * D + 512 * (dh + 1)],
                                    pY[pp + dh][:, :],
                                    gates[:, j * E + e_i: j * E + e_i + 1],
                                    acc[:, j * D + 512 * dh: j * D + 512 * (dh + 1)],
                                    op0=ALU.mult, op1=ALU.add)
                            return ins
                        vga = EMIT("v", _ga, "tV")
                        for dh in range(2):
                            bank_done("v", "pY%d" % (pp + dh), "tV", vga)

            for ek in ["t", "s", "v"]:
                W(ek, OWN[ek], P.sem_val[OWN[ek]])
                P.add(ek, lambda e: e.nop().then_inc(sem["ph"], 1))
                P.inc("ph")
            W("y", "dA", P.sem_val["dA"])
            W("y", "dB0", 16 * b1_loads[0])
            W("y", "dB1", 16 * b1_loads[1])
            P.add("y", lambda e: e.nop().then_inc(sem["ph"], 1))
            P.inc("ph")
            for sn in ["dW1a", "dW1b", "dW2a", "dW2b", "dB2a", "dB2b"]:
                W("g", sn, P.sem_val.get(sn, 0))
            P.add("g", lambda e: e.nop().then_inc(sem["ph"], 1))
            bar_f = P.inc("ph")
        for ek in ["t", "s", "v", "g", "y"]:
            W(ek, "ph", bar_f)

        with contextlib.ExitStack() as fo:
            x_sb2 = fo.enter_context(nc.sbuf_tensor("x_sb2", [128, 2 * D], F32))
            slot_free = [None, None]
            DR("v")
            for j in range(NT):
                slot = j % 2
                dxs = "dX0" if slot == 0 else "dX1"
                if slot_free[slot] is not None:
                    W("y", slot_free[slot][0], slot_free[slot][1])
                vload = EMIT("y", lambda e, j=j, slot=slot: e.dma_start(
                    out=x_sb2[:, slot * D:(slot + 1) * D],
                    in_=x_h.rearrange("(j p) d -> j p d", p=128)[j]), dxs, 16)
                W("v", dxs, vload)
                vadd = EMIT("v", lambda e, j=j, slot=slot: e.tensor_add(
                    acc[:, j * D:(j + 1) * D],
                    acc[:, j * D:(j + 1) * D],
                    x_sb2[:, slot * D:(slot + 1) * D]), "tV")
                DR("v")
                slot_free[slot] = ("tV", vadd)
                W("y", "tV", vadd)
                EMIT("y", lambda e, j=j: e.dma_start(
                    out=out_h.rearrange("(j p) d -> j p d", p=128)[j],
                    in_=acc[:, j * D:(j + 1) * D]), "dOut", 16)
            W("y", "dOut", P.sem_val["dOut"])

        @block.tensor
        def _(eng):
            for fn in P.steps["t"]:
                fn(eng)

        @block.scalar
        def _(eng):
            for fn in P.steps["s"]:
                fn(eng)

        @block.vector
        def _(eng):
            for fn in P.steps["v"]:
                fn(eng)

        @block.gpsimd
        def _(eng):
            for fn in P.steps["g"]:
                fn(eng)

        @block.sync
        def _(eng):
            for fn in P.steps["y"]:
                fn(eng)

    return nc


def kernel(L0_attn_x, L0_attn_wq, L0_attn_wk, L0_attn_wv, L0_attn_wo,
           L0_ffn_Wr, L0_ffn_br, L0_ffn_W1, L0_ffn_b1, L0_ffn_W2, L0_ffn_b2):
    x = np.ascontiguousarray(np.asarray(L0_attn_x, dtype=np.float32))
    B = x.shape[0]
    shared = dict(
        wq=np.ascontiguousarray(np.asarray(L0_attn_wq, np.float32)),
        wk=np.ascontiguousarray(np.asarray(L0_attn_wk, np.float32)),
        wv=np.ascontiguousarray(np.asarray(L0_attn_wv, np.float32)),
        wo=np.ascontiguousarray(np.asarray(L0_attn_wo, np.float32)),
        wr=np.ascontiguousarray(np.asarray(L0_ffn_Wr, np.float32)),
        br=np.ascontiguousarray(np.asarray(L0_ffn_br, np.float32)).reshape(1, E),
        w1=np.ascontiguousarray(np.asarray(L0_ffn_W1, np.float32)),
        b1=np.ascontiguousarray(np.asarray(L0_ffn_b1, np.float32)),
        w2=np.ascontiguousarray(np.asarray(L0_ffn_W2, np.float32)),
        b2=np.ascontiguousarray(np.asarray(L0_ffn_b2, np.float32)),
        ident=np.eye(128, dtype=np.float32),
        tril=np.triu(np.ones((128, 128), np.float32)),
    )
    nc = build_nc()
    in_maps = [dict(x=x[c], **shared) for c in range(B)]
    res = run_bass_kernel_spmd(nc, in_maps, core_ids=list(range(B)))
    out = np.stack([res.results[c]["out"] for c in range(B)], axis=0)
    return out


# version 03
# speedup vs baseline: 1.1760x; 1.1760x over previous
import contextlib
import numpy as np
import concourse.bass as bass
import concourse.mybir as mybir
from concourse.bass_utils import run_bass_kernel_spmd

F32 = mybir.dt.float32
BF16 = mybir.dt.bfloat16
AF = mybir.ActivationFunctionType
ALU = mybir.AluOpType
AX = mybir.AxisListType

S = 2048
D = 1024
NH = 16
E = 8
NT = S // 128
ND = D // 128
NQC = S // 512


class Plan:
    def __init__(self):
        self.steps = {"t": [], "s": [], "v": [], "g": [], "y": []}
        self.sem_val = {}

    def add(self, eng, fn):
        self.steps[eng].append(fn)

    def inc(self, sem_name, amount=1):
        v = self.sem_val.get(sem_name, 0) + amount
        self.sem_val[sem_name] = v
        return v


def build_nc():
    nc = bass.Bass("TRN2", target_bir_lowering=False)

    x_h = nc.dram_tensor("x", [S, D], F32, kind="ExternalInput").ap()
    wq_h = nc.dram_tensor("wq", [D, D], F32, kind="ExternalInput").ap()
    wk_h = nc.dram_tensor("wk", [D, D], F32, kind="ExternalInput").ap()
    wv_h = nc.dram_tensor("wv", [D, D], F32, kind="ExternalInput").ap()
    wo_h = nc.dram_tensor("wo", [D, D], F32, kind="ExternalInput").ap()
    wr_h = nc.dram_tensor("wr", [D, E], F32, kind="ExternalInput").ap()
    br_h = nc.dram_tensor("br", [1, E], F32, kind="ExternalInput").ap()
    w1_h = nc.dram_tensor("w1", [E, D, 4096], F32, kind="ExternalInput").ap()
    b1_h = nc.dram_tensor("b1", [E, 4096], F32, kind="ExternalInput").ap()
    w2_h = nc.dram_tensor("w2", [E, 4096, D], F32, kind="ExternalInput").ap()
    b2_h = nc.dram_tensor("b2", [E, D], F32, kind="ExternalInput").ap()
    id_h = nc.dram_tensor("ident", [128, 128], F32, kind="ExternalInput").ap()
    tril_h = nc.dram_tensor("tril", [128, 128], F32, kind="ExternalInput").ap()
    out_h = nc.dram_tensor("out", [S, D], F32, kind="ExternalOutput").ap()
    xnts_h = nc.dram_tensor("xnts", [128, ND * S], BF16, kind="Internal").ap()

    P = Plan()
    SEMS = ["dX0", "dX1", "dA", "dC", "dWq", "dWo", "dW1a", "dW1b",
            "dW2a", "dW2b", "dB0", "dB1", "dB2a", "dB2b", "dOut",
            "tT", "tS", "tV", "ph"]
    OWN = {"t": "tT", "s": "tS", "v": "tV"}

    with contextlib.ExitStack() as ctx:
        sem = {n: ctx.enter_context(nc.semaphore(n)) for n in SEMS}

        def W(eng_key, sname, val):
            if val > 0:
                P.add(eng_key, lambda e, s=sem[sname], v=val: e.wait_ge(s, v))

        def DR(eng_key):
            P.add(eng_key, lambda e: e.drain())

        def EMIT(eng_key, fn, inc_sem=None, inc_by=1):
            if inc_sem is None:
                P.add(eng_key, fn)
                return None
            v = P.inc(inc_sem, inc_by)
            P.add(eng_key, lambda e, f=fn, s=sem[inc_sem], a=inc_by: f(e).then_inc(s, a))
            return v

        bank_last = {}

        def bank_wait(eng_key, bank):
            la = bank_last.get(bank)
            if la is not None and la[0] != eng_key:
                W(eng_key, la[1], la[2])

        def bank_done(eng_key, bank, sname, val):
            bank_last[bank] = (eng_key, sname, val)

        gates = ctx.enter_context(nc.sbuf_tensor("gates", [128, NT * E], F32))
        id_f = ctx.enter_context(nc.sbuf_tensor("id_f", [128, 128], F32))
        id_b = ctx.enter_context(nc.sbuf_tensor("id_b", [128, 128], BF16))
        tril_b = ctx.enter_context(nc.sbuf_tensor("tril_b", [128, 128], BF16))
        br_t = ctx.enter_context(nc.sbuf_tensor("br_t", [128, E], F32))
        ones_b = ctx.enter_context(nc.sbuf_tensor("ones_b", [1, 128], BF16))
        eps_t = ctx.enter_context(nc.sbuf_tensor("eps_t", [128, 1], F32))
        lg_all = ctx.enter_context(nc.sbuf_tensor("lg_all", [128, NT * E], F32))
        rt8 = ctx.enter_context(nc.sbuf_tensor("rt8", [128, 6 * E], F32))
        rs1 = ctx.enter_context(nc.sbuf_tensor("rs1", [128, 8], F32))
        wr_sb = ctx.enter_context(nc.sbuf_tensor("wr_sb", [128, ND * E], BF16))
        qkv_scope = ctx.enter_context(contextlib.ExitStack())
        qT = qkv_scope.enter_context(nc.sbuf_tensor("qT", [128, ND * S], BF16))
        kT = qkv_scope.enter_context(nc.sbuf_tensor("kT", [128, ND * S], BF16))
        v_sb = qkv_scope.enter_context(nc.sbuf_tensor("v_sb", [128, NT * NH * 65], BF16))
        acc_scope = ctx.enter_context(contextlib.ExitStack())
        acc = None

        block = ctx.enter_context(nc.Block())

        EMIT("y", lambda e: e.dma_start(out=id_f[:, :], in_=id_h), "dA", 16)
        EMIT("g", lambda e: e.dma_start(out=id_b[:, :], in_=id_h), "dC", 16)
        EMIT("g", lambda e: e.dma_start(out=tril_b[:, :], in_=tril_h), "dC", 16)
        EMIT("g", lambda e: e.dma_start(out=br_t[:, :], in_=br_h.partition_broadcast(128)), "dC", 16)
        EMIT("g", lambda e: e.dma_start(
            out=wr_sb[:, :].rearrange("p (t e) -> p t e", t=ND),
            in_=wr_h.rearrange("(t p) e -> p t e", p=128)), "dC", 16)
        vC = P.sem_val["dC"]
        EMIT("v", lambda e: e.memset(ones_b[:, :], 1.0), "tV")
        veps = EMIT("v", lambda e: e.memset(eps_t[:, :], 1e-5), "tV")
        EMIT("v", lambda e: e.memset(
            v_sb[:, :].rearrange("p (g c) -> p g c", c=65)[:, :, 64:65], 1.0), "tV")

        with contextlib.ExitStack() as a1:
            xnT = a1.enter_context(nc.sbuf_tensor("xnT", [128, ND * S], BF16))
            x_sb = a1.enter_context(nc.sbuf_tensor("x_sb", [128, 2 * D], F32))
            xnb = a1.enter_context(nc.sbuf_tensor("xnb", [128, D], BF16))
            bs_t = a1.enter_context(nc.sbuf_tensor("bs_t", [128, 12], F32))
            ba_t = a1.enter_context(nc.sbuf_tensor("ba_t", [128, 2], F32))
            st1 = a1.enter_context(nc.sbuf_tensor("st1", [128, 8], F32))
            w_sb = a1.enter_context(nc.sbuf_tensor("w_sb", [128, ND * D], BF16))

            pbT = a1.enter_context(nc.psum_tensor("pbT", [128, 1024], BF16))
            pb0 = a1.enter_context(nc.psum_tensor("pb0", [128, 512], F32))
            pb1 = a1.enter_context(nc.psum_tensor("pb1", [128, 512], F32))
            pb2 = a1.enter_context(nc.psum_tensor("pb2", [128, 512], F32))

            EMIT("g", lambda e: e.dma_start(
                out=w_sb[:, :].rearrange("p (t f) -> p t f", t=ND),
                in_=wq_h.rearrange("(t p) f -> p t f", p=128)), "dWq", 16)

            x_slot_free = [None, None]
            last_xn_evac = None
            last_tr = None

            for j in range(NT):
                slot = j % 2
                dxs = "dX0" if slot == 0 else "dX1"
                if x_slot_free[slot] is not None:
                    W("y", x_slot_free[slot][0], x_slot_free[slot][1])
                vload = EMIT("y", lambda e, j=j, slot=slot: e.dma_start(
                    out=x_sb[:, slot * D:(slot + 1) * D],
                    in_=x_h.rearrange("(j p) d -> j p d", p=128)[j]), dxs, 16)
                W("v", dxs, vload)
                EMIT("v", lambda e, slot=slot: e.bn_stats(
                    bs_t[:, 0:6], x_sb[:, slot * D: slot * D + 512]))
                EMIT("v", lambda e, slot=slot: e.bn_stats(
                    bs_t[:, 6:12], x_sb[:, slot * D + 512:(slot + 1) * D]))
                DR("v")
                vvar = EMIT("v", lambda e: e.bn_aggr(ba_t[:, :], bs_t[:, :]), "tV")
                W("s", "tV", vvar)
                EMIT("s", lambda e: e.activation(st1[:, 2:3], ba_t[:, 1:2], AF.Ln,
                                                 bias=eps_t[:, 0:1], scale=1.0))
                DR("s")
                vrstd = EMIT("s", lambda e: e.activation(st1[:, 3:4], st1[:, 2:3], AF.Exp,
                                                         scale=-0.5), "tS")
                W("v", "tS", vrstd)
                if last_tr is not None:
                    W("v", "tT", last_tr)
                DR("v")
                vxn = EMIT("v", lambda e, slot=slot: e.tensor_scalar(
                    xnb[:, :], x_sb[:, slot * D:(slot + 1) * D],
                    ba_t[:, 0:1], st1[:, 3:4], op0=ALU.subtract, op1=ALU.mult), "tV")
                x_slot_free[slot] = ("tV", vxn)
                W("t", "tV", vxn)
                W("t", "dC", vC)
                bank_wait("t", "pbT")
                def _tr(e, j=j):
                    ins = None
                    for d in range(ND):
                        ins = e.transpose(pbT[:, d * 128:(d + 1) * 128],
                                          xnb[:, d * 128:(d + 1) * 128], id_b[:, :])
                    return ins
                vtr = EMIT("t", _tr, "tT")
                last_tr = vtr
                bank_done("t", "pbT", "tT", vtr)
                bank_wait("v", "pbT")
                def _ev(e, j=j):
                    ins = None
                    for d in range(ND):
                        ins = e.tensor_copy(xnT[:, d * S + 128 * j: d * S + 128 * (j + 1)],
                                            pbT[:, d * 128:(d + 1) * 128])
                    return ins
                vev = EMIT("v", _ev, "tV")
                bank_done("v", "pbT", "tV", vev)
                last_xn_evac = vev

                W("t", "tV", vev)
                bank_wait("t", "pb2")
                def _rt(e, j=j):
                    ins = None
                    for d in range(ND):
                        ins = e.matmul(pb2[:, 0:E],
                                       xnT[:, d * S + 128 * j: d * S + 128 * (j + 1)],
                                       wr_sb[:, d * E:(d + 1) * E],
                                       start=(d == 0), stop=(d == ND - 1))
                    return ins
                vrt = EMIT("t", _rt, "tT")
                bank_done("t", "pb2", "tT", vrt)
                bank_wait("v", "pb2")
                vpb2 = EMIT("v", lambda e, j=j: e.tensor_add(
                    lg_all[:, j * E:(j + 1) * E], pb2[:, 0:E], br_t[:, :]), "tV")
                bank_done("v", "pb2", "tV", vpb2)
                DR("v")

            def proj_qk(dst, dsem_val):
                for u in range(ND):
                    for scq in range(NQC):
                        par = (u * NQC + scq) % 2
                        pb = pb0 if par == 0 else pb1
                        pbn = "pb0" if par == 0 else "pb1"
                        W("t", "dWq", dsem_val)
                        if u == 0 and scq == 0:
                            W("t", "tV", last_xn_evac)
                        bank_wait("t", pbn)
                        def _mm(e, u=u, scq=scq, pb=pb):
                            ins = None
                            for d in range(ND):
                                ins = e.matmul(pb[:, :],
                                               w_sb[:, d * D + 128 * u: d * D + 128 * (u + 1)],
                                               xnT[:, d * S + 512 * scq: d * S + 512 * (scq + 1)],
                                               start=(d == 0), stop=(d == ND - 1))
                            return ins
                        vmm = EMIT("t", _mm, "tT")
                        bank_done("t", pbn, "tT", vmm)
                        ekey = "s" if par == 0 else "v"
                        bank_wait(ekey, pbn)
                        if ekey == "s":
                            vev = EMIT("s", lambda e, u=u, scq=scq, pb=pb, dst=dst: e.activation(
                                dst[:, u * S + 512 * scq: u * S + 512 * (scq + 1)],
                                pb[:, :], AF.Copy), "tS")
                        else:
                            vev = EMIT("v", lambda e, u=u, scq=scq, pb=pb, dst=dst: e.tensor_copy(
                                dst[:, u * S + 512 * scq: u * S + 512 * (scq + 1)],
                                pb[:, :]), "tV")
                        bank_done(ekey, pbn, OWN[ekey], vev)

            proj_qk(qT, 16)
            W("g", "tT", P.sem_val["tT"])
            vwk = EMIT("g", lambda e: e.dma_start(
                out=w_sb[:, :].rearrange("p (t f) -> p t f", t=ND),
                in_=wk_h.rearrange("(t p) f -> p t f", p=128)), "dWq", 16)
            proj_qk(kT, vwk)
            W("g", "tT", P.sem_val["tT"])
            vwv = EMIT("g", lambda e: e.dma_start(
                out=w_sb[:, :].rearrange("p (t f) -> p t f", t=ND),
                in_=wv_h.rearrange("(t p) f -> p t f", p=128)), "dWq", 16)
            for j in range(NT):
                for ch in range(2):
                    par = (j * 2 + ch) % 2
                    pb = pb0 if par == 0 else pb1
                    pbn = "pb0" if par == 0 else "pb1"
                    W("t", "dWq", vwv)
                    bank_wait("t", pbn)
                    def _mmv(e, j=j, ch=ch, pb=pb):
                        ins = None
                        for d in range(ND):
                            ins = e.matmul(pb[:, :],
                                           xnT[:, d * S + 128 * j: d * S + 128 * (j + 1)],
                                           w_sb[:, d * D + 512 * ch: d * D + 512 * (ch + 1)],
                                           start=(d == 0), stop=(d == ND - 1))
                        return ins
                    vmm = EMIT("t", _mmv, "tT")
                    bank_done("t", pbn, "tT", vmm)
                    ekey = "s" if ch == 0 else "v"
                    bank_wait(ekey, pbn)
                    base = j * NH * 65 + ch * 8 * 65
                    if ekey == "s":
                        vev = EMIT("s", lambda e, base=base, pb=pb: e.activation(
                            v_sb[:, base:base + 8 * 65].rearrange("p (h c) -> p h c", c=65)[:, :, 0:64],
                            pb[:, :].rearrange("p (h c) -> p h c", c=64), AF.Copy), "tS")
                    else:
                        vev = EMIT("v", lambda e, base=base, pb=pb: e.tensor_copy(
                            v_sb[:, base:base + 8 * 65].rearrange("p (h c) -> p h c", c=65)[:, :, 0:64],
                            pb[:, :].rearrange("p (h c) -> p h c", c=64)), "tV")
                    bank_done(ekey, pbn, OWN[ekey], vev)

            W("y", "tV", P.sem_val["tV"])
            W("y", "tS", P.sem_val["tS"])
            EMIT("y", lambda e: e.dma_start(out=xnts_h, in_=xnT[:, :]), "dA", 16)

            for ek in ["t", "s", "v"]:
                W(ek, OWN[ek], P.sem_val[OWN[ek]])
                P.add(ek, lambda e: e.nop().then_inc(sem["ph"], 1))
                P.inc("ph")
            W("y", "dA", P.sem_val["dA"])
            W("y", "dX0", P.sem_val["dX0"])
            W("y", "dX1", P.sem_val["dX1"])
            P.add("y", lambda e: e.nop().then_inc(sem["ph"], 1))
            P.inc("ph")
            W("g", "dWq", P.sem_val["dWq"])
            W("g", "dC", P.sem_val["dC"])
            P.add("g", lambda e: e.nop().then_inc(sem["ph"], 1))
            bar_a1 = P.inc("ph")
        for ek in ["t", "s", "v", "g", "y"]:
            W(ek, "ph", bar_a1)
        bank_last.clear()
        acc = acc_scope.enter_context(nc.sbuf_tensor("acc", [128, NT * D], F32, side="right"))

        with contextlib.ExitStack() as a2:
            pT = a2.enter_context(nc.sbuf_tensor("pT", [128, 16 * 512], BF16))
            ctx_all = a2.enter_context(nc.sbuf_tensor("ctx_all", [128, 8 * 512], BF16))
            wo_sb = a2.enter_context(nc.sbuf_tensor("wo_sb", [128, 8 * 1024], BF16))
            rrow_f = a2.enter_context(nc.sbuf_tensor("rrow_f", [1, 512], F32))
            rrow_b = a2.enter_context(nc.sbuf_tensor("rrow_b", [1, 512], BF16))
            bc_sb = a2.enter_context(nc.sbuf_tensor("bc_sb", [64, 512], BF16))

            pL = [a2.enter_context(nc.psum_tensor("pL%d" % i, [128, 512], F32))
                  for i in range(4)]
            pC = [a2.enter_context(nc.psum_tensor("pC0", [65, 512], F32)),
                  a2.enter_context(nc.psum_tensor("pC1", [65, 512], F32))]
            pB = a2.enter_context(nc.psum_tensor("pB", [64, 512], F32))
            pO = [pL[0], pL[1]]

            vwo = EMIT("g", lambda e: e.dma_start(
                out=wo_sb[:, :].rearrange("p (r d) -> p r d", r=8),
                in_=wo_h.rearrange("(r p) d -> p r d", p=128)), "dWo", 16)

            lctr = 0
            pv_reads = {}
            gate_vals = {}
            wo_waited = False

            def emit_gates(j):
                lg = lg_all[:, j * E:(j + 1) * E]
                eg = rt8[:, 1 * E:2 * E]
                ind = rt8[:, 2 * E:3 * E]
                ind2 = rt8[:, 3 * E:4 * E]
                u = rt8[:, 4 * E:5 * E]
                p_ = rt8[:, 5 * E:6 * E]
                DR("v")
                EMIT("v", lambda e, lg=lg: e.reduce_max(rs1[:, 0:1], lg, axis=AX.X, negate=True))
                DR("v")
                EMIT("v", lambda e, lg=lg, ind=ind: e.tensor_scalar(
                    ind, lg, rs1[:, 0:1], 0.0, op0=ALU.add, op1=ALU.is_ge))
                DR("v")
                EMIT("v", lambda e, lg=lg, ind=ind, u=u: e.scalar_tensor_tensor(
                    u, ind, -1e30, lg, op0=ALU.mult, op1=ALU.add))
                DR("v")
                vnm2 = EMIT("v", lambda e, u=u: e.reduce_max(
                    rs1[:, 1:2], u, axis=AX.X, negate=True), "tV")
                W("s", "tV", vnm2)
                DR("s")
                vexp = EMIT("s", lambda e, lg=lg, eg=eg: e.activation(
                    eg, lg, AF.Exp, bias=rs1[:, 0:1], scale=1.0), "tS")
                W("v", "tS", vexp)
                EMIT("v", lambda e, lg=lg, ind2=ind2: e.tensor_scalar(
                    ind2, lg, rs1[:, 1:2], 0.0, op0=ALU.add, op1=ALU.is_ge))
                DR("v")
                EMIT("v", lambda e, eg=eg, ind2=ind2, p_=p_: e.tensor_mul(p_, eg, ind2))
                DR("v")
                EMIT("v", lambda e, p_=p_: e.reduce_sum(rs1[:, 2:3], p_, axis=AX.X))
                DR("v")
                EMIT("v", lambda e: e.reciprocal(rs1[:, 3:4], rs1[:, 2:3]))
                DR("v")
                EMIT("v", lambda e, j=j, p_=p_: e.tensor_scalar_mul(
                    gates[:, j * E:(j + 1) * E], p_, rs1[:, 3:4]), "tV")
                DR("v")

            DEPTH = 3
            for qc in range(NQC):
                ctx_ready = []
                for h in range(NH):
                    hp = h % 2
                    hb = h // 2
                    cpn = "pC%d" % hp
                    nt_ = 4 * qc + 4
                    for i in range(nt_ + DEPTH):
                        if i < nt_:
                            t = i
                            qv0 = max(0, 128 * t - 512 * qc)
                            lb = lctr % 4
                            lctr += 1
                            lbn = "pL%d" % lb
                            bank_wait("t", lbn)
                            vlg = EMIT("t", lambda e, t=t, qc=qc, hp=hp, hb=hb, qv0=qv0, lb=lb:
                                       e.matmul(pL[lb][:, qv0:512],
                                                kT[64 * hp:64 * hp + 64,
                                                   hb * S + 128 * t: hb * S + 128 * (t + 1)],
                                                qT[64 * hp:64 * hp + 64,
                                                   hb * S + 512 * qc + qv0: hb * S + 512 * (qc + 1)],
                                                start=True, stop=True), "tT")
                            bank_done("t", lbn, "tT", vlg)
                            bank_wait("s", lbn)
                            if t in pv_reads:
                                W("s", "tT", pv_reads[t])
                            vex = EMIT("s", lambda e, t=t, qv0=qv0, lb=lb: e.activation(
                                pT[:, 512 * t + qv0: 512 * (t + 1)],
                                pL[lb][:, qv0:512], AF.Exp, scale=0.125), "tS")
                            bank_done("s", lbn, "tS", vex)
                            gate_vals[t] = ("tS", vex)
                            if t >= 4 * qc:
                                W("v", "tS", vex)
                                vmask = EMIT("v", lambda e, t=t, qv0=qv0: e.tensor_mul(
                                    pT[:, 512 * t + qv0: 512 * t + qv0 + 128],
                                    pT[:, 512 * t + qv0: 512 * t + qv0 + 128],
                                    tril_b[:, :]), "tV")
                                DR("v")
                                gate_vals[t] = ("tV", vmask)
                        if i >= DEPTH:
                            t = i - DEPTH
                            qv0 = max(0, 128 * t - 512 * qc)
                            gv = gate_vals[t]
                            W("t", gv[0], gv[1])
                            if t == 0:
                                bank_wait("t", cpn)
                            vpv = EMIT("t", lambda e, t=t, h=h, hp=hp, qv0=qv0,
                                       first=(t == 0), last=(t == nt_ - 1):
                                       e.matmul(pC[hp][:, qv0:512],
                                                v_sb[:, t * NH * 65 + 65 * h: t * NH * 65 + 65 * h + 65],
                                                pT[:, 512 * t + qv0: 512 * (t + 1)],
                                                start=first, stop=last), "tT")
                            pv_reads[t] = vpv
                    bank_done("t", cpn, "tT", vpv)
                    bank_wait("s", cpn)
                    DR("s")
                    EMIT("s", lambda e, hp=hp: e.activation(rrow_f[:, :], pC[hp][64:65, :], AF.Ln))
                    DR("s")
                    vrb = EMIT("s", lambda e: e.activation(rrow_b[:, :], rrow_f[:, :], AF.Exp,
                                                           scale=-1.0), "tS")
                    bank_done("s", cpn, "tS", vrb)
                    W("t", "tS", vrb)
                    bank_wait("t", "pB")
                    vbc = EMIT("t", lambda e: e.matmul(pB[0:64, :], ones_b[:, 0:64], rrow_b[:, :],
                                                       start=True, stop=True), "tT")
                    bank_done("t", "pB", "tT", vbc)
                    bank_wait("v", "pB")
                    EMIT("v", lambda e: e.tensor_copy(bc_sb[:, :], pB[0:64, :]))
                    DR("v")
                    vnz = EMIT("v", lambda e, hp=hp, hb=hb: e.tensor_mul(
                        ctx_all[64 * hp:64 * hp + 64, 512 * hb: 512 * (hb + 1)],
                        pC[hp][0:64, :], bc_sb[:, :]), "tV")
                    DR("v")
                    bank_done("v", "pB", "tV", vnz)
                    bank_done("v", cpn, "tV", vnz)
                    ctx_ready.append(vnz)
                    if qc == 0:
                        emit_gates(h)
                if not wo_waited:
                    W("t", "dWo", vwo)
                    wo_waited = True
                for j4 in range(4):
                    for half in range(2):
                        bank_wait("t", "pL0")
                        bank_wait("t", "pL1")
                        W("t", "tV", ctx_ready[-1])
                        def _op(e, j4=j4, half=half):
                            ins = None
                            for h in range(NH):
                                hp = h % 2
                                hb = h // 2
                                ins = e.matmul(
                                    pO[hp][:, :],
                                    ctx_all[64 * hp:64 * hp + 64,
                                            512 * hb + 128 * j4: 512 * hb + 128 * (j4 + 1)],
                                    wo_sb[64 * hp:64 * hp + 64,
                                          1024 * hb + 512 * half: 1024 * hb + 512 * (half + 1)],
                                    start=(h < 2), stop=(h >= NH - 2))
                            return ins
                        vop = EMIT("t", _op, "tT")
                        bank_done("t", "pL0", "tT", vop)
                        bank_done("t", "pL1", "tT", vop)
                        bank_wait("v", "pL0")
                        bank_wait("v", "pL1")
                        jg = 4 * qc + j4
                        EMIT("v", lambda e, jg=jg, half=half: e.tensor_copy(
                            acc[:, jg * D + 512 * half: jg * D + 512 * (half + 1)], pO[0][:, :]))
                        DR("v")
                        voe = EMIT("v", lambda e, jg=jg, half=half: e.tensor_add(
                            acc[:, jg * D + 512 * half: jg * D + 512 * (half + 1)],
                            acc[:, jg * D + 512 * half: jg * D + 512 * (half + 1)],
                            pO[1][:, :]), "tV")
                        DR("v")
                        bank_done("v", "pL0", "tV", voe)
                        bank_done("v", "pL1", "tV", voe)

            for ek in ["t", "s", "v"]:
                W(ek, OWN[ek], P.sem_val[OWN[ek]])
                P.add(ek, lambda e: e.nop().then_inc(sem["ph"], 1))
                P.inc("ph")
            W("y", "dA", P.sem_val["dA"])
            P.add("y", lambda e: e.nop().then_inc(sem["ph"], 1))
            P.inc("ph")
            W("g", "dWo", P.sem_val["dWo"])
            P.add("g", lambda e: e.nop().then_inc(sem["ph"], 1))
            bar_a2 = P.inc("ph")
        for ek in ["t", "s", "v", "g", "y"]:
            W(ek, "ph", bar_a2)
        bank_last.clear()
        qkv_scope.close()

        with contextlib.ExitStack() as f:
            xnT2 = f.enter_context(nc.sbuf_tensor("xnT2", [128, ND * S], BF16))
            hT = f.enter_context(nc.sbuf_tensor("hT", [128, 8 * S], BF16))
            w2q = f.enter_context(nc.sbuf_tensor("w2q", [128, 2 * 8 * D], BF16))
            w1r = f.enter_context(nc.sbuf_tensor("w1r", [128, 2 * ND * 512], BF16))
            b1st = f.enter_context(nc.sbuf_tensor("b1st", [32, 2 * 128], F32))
            b1sb = f.enter_context(nc.sbuf_tensor("b1sb", [128, 2 * 32], F32))
            b2r = f.enter_context(nc.sbuf_tensor("b2r", [1, 2 * D], BF16))

            pH = [f.enter_context(nc.psum_tensor("pH%d" % i, [128, 512], F32)) for i in range(4)]
            pY = [f.enter_context(nc.psum_tensor("pY%d" % i, [128, 512], F32)) for i in range(4)]

            vxl = EMIT("y", lambda e: e.dma_start(out=xnT2[:, :], in_=xnts_h), "dA", 16)
            W("t", "dA", vxl)
            DR("v")

            w1_free = [None, None]
            w2_free = [None, None]
            b1_free = [None, None]
            b2_free = [None, None]
            w1_loads = [0, 0]
            w2_loads = [0, 0]
            b1_loads = [0, 0]
            b2_loads = [0, 0]
            w1_idx = 0

            for e_i in range(E):
                es = e_i % 2
                if b1_free[es] is not None:
                    W("y", b1_free[es][0], b1_free[es][1])
                b1_loads[es] += 1
                EMIT("y", lambda e, e_i=e_i, es=es: e.dma_start(
                    out=b1st[:, es * 128:(es + 1) * 128],
                    in_=b1_h[e_i].rearrange("(a b) -> a b", b=128)), "dB%d" % es, 16)
                if b2_free[es] is not None:
                    W("g", b2_free[es][0], b2_free[es][1])
                b2_loads[es] += 1
                EMIT("g", lambda e, e_i=e_i, es=es: e.dma_start(
                    out=b2r[:, es * D:(es + 1) * D],
                    in_=b2_h[e_i:e_i + 1, :]), "dB2%s" % ("a" if es == 0 else "b"), 16)
                W("t", "dB%d" % es, 16 * b1_loads[es])
                bank_wait("t", "pH0")
                vtr = EMIT("t", lambda e, es=es: e.transpose(
                    pH[0][:, 0:32], b1st[:, es * 128:(es + 1) * 128],
                    id_f[0:32, 0:32]), "tT")
                bank_done("t", "pH0", "tT", vtr)
                b1_free[es] = ("tT", vtr)
                bank_wait("v", "pH0")
                DR("v")
                vb1e = EMIT("v", lambda e, es=es: e.tensor_copy(
                    b1sb[:, es * 32:(es + 1) * 32], pH[0][:, 0:32]), "tV")
                DR("v")
                bank_done("v", "pH0", "tV", vb1e)
                b1_ready = vb1e

                for q in range(4):
                    ws = (e_i * 4 + q) % 2
                    if w2_free[ws] is not None:
                        W("g", w2_free[ws][0], w2_free[ws][1])
                    w2_loads[ws] += 1
                    EMIT("g", lambda e, e_i=e_i, q=q, ws=ws: e.dma_start(
                        out=w2q[:, ws * 8 * D:(ws + 1) * 8 * D].rearrange("p (t d) -> p t d", t=8),
                        in_=w2_h[e_i].rearrange("(qq t p) d -> qq p t d", qq=4, p=128)[q]),
                        "dW2%s" % ("a" if ws == 0 else "b"), 16)
                    w1v = []
                    for hh in range(2):
                        s_ = w1_idx % 2
                        w1_idx += 1
                        if w1_free[s_] is not None:
                            W("g", w1_free[s_][0], w1_free[s_][1])
                        w1_loads[s_] += 1
                        EMIT("g", lambda e, e_i=e_i, q=q, hh=hh, s_=s_: e.dma_start(
                            out=w1r[:, s_ * ND * 512:(s_ + 1) * ND * 512].rearrange(
                                "p (t f) -> p t f", t=ND),
                            in_=w1_h[e_i].rearrange("(t p) f -> p t f", p=128)[
                                :, :, 1024 * q + 512 * hh: 1024 * q + 512 * (hh + 1)]),
                            "dW1%s" % ("a" if s_ == 0 else "b"), 16)
                        w1v.append((s_, 16 * w1_loads[s_]))

                    hT_evacs = []
                    for ft in range(8):
                        hh = ft // 4
                        s_, wv_ = w1v[hh]
                        fo = (ft % 4) * 128
                        W("t", "dW1%s" % ("a" if s_ == 0 else "b"), wv_)
                        for sc in range(NQC):
                            bank_wait("t", "pH%d" % sc)
                        def _w1(e, s_=s_, fo=fo):
                            ins = None
                            for d in range(ND):
                                for sc in range(NQC):
                                    ins = e.matmul(
                                        pH[sc][:, :],
                                        w1r[:, s_ * ND * 512 + d * 512 + fo:
                                            s_ * ND * 512 + d * 512 + fo + 128],
                                        xnT2[:, d * S + 512 * sc: d * S + 512 * (sc + 1)],
                                        start=(d == 0), stop=(d == ND - 1))
                            return ins
                        vw1m = EMIT("t", _w1, "tT")
                        for sc in range(NQC):
                            bank_done("t", "pH%d" % sc, "tT", vw1m)
                        if ft % 4 == 3:
                            w1_free[s_] = ("tT", vw1m)
                        for sc in range(NQC):
                            ekey = "s" if sc < 2 else "v"
                            bank_wait(ekey, "pH%d" % sc)
                            if ekey == "s":
                                W("s", "tV", b1_ready)
                                vev = EMIT("s", lambda e, sc=sc, ft=ft, es=es, q=q: e.activation(
                                    hT[:, ft * S + 512 * sc: ft * S + 512 * (sc + 1)],
                                    pH[sc][:, :], AF.Relu,
                                    bias=b1sb[:, es * 32 + 8 * q + ft:
                                              es * 32 + 8 * q + ft + 1]), "tS")
                            else:
                                vev = EMIT("v", lambda e, sc=sc, ft=ft, es=es, q=q: e.tensor_scalar(
                                    hT[:, ft * S + 512 * sc: ft * S + 512 * (sc + 1)],
                                    pH[sc][:, :],
                                    b1sb[:, es * 32 + 8 * q + ft: es * 32 + 8 * q + ft + 1], 0.0,
                                    op0=ALU.add, op1=ALU.max), "tV")
                            bank_done(ekey, "pH%d" % sc, OWN[ekey], vev)
                            hT_evacs.append((ekey, vev))

                    W("t", "dW2%s" % ("a" if ws == 0 else "b"), 16 * w2_loads[ws])
                    if q == 0:
                        W("t", "dB2%s" % ("a" if es == 0 else "b"), 16 * b2_loads[es])
                    DR("v")
                    for j in range(NT):
                        pp = (j % 2) * 2
                        for dh in range(2):
                            bank_wait("t", "pY%d" % (pp + dh))
                        if j == 0:
                            for (ekey, vev) in hT_evacs:
                                W("t", OWN[ekey], vev)
                        def _w2(e, ws=ws, es=es, q=q, j=j, pp=pp):
                            ins = None
                            for ft in range(8):
                                for dh in range(2):
                                    ins = e.matmul(
                                        pY[pp + dh][:, :],
                                        hT[:, ft * S + 128 * j: ft * S + 128 * (j + 1)],
                                        w2q[:, ws * 8 * D + ft * D + 512 * dh:
                                            ws * 8 * D + ft * D + 512 * (dh + 1)],
                                        start=(ft == 0), stop=(ft == 7 and q != 0))
                            if q == 0:
                                for dh in range(2):
                                    ins = e.matmul(
                                        pY[pp + dh][:, :],
                                        ones_b[:, 0:128],
                                        b2r[:, es * D + 512 * dh: es * D + 512 * (dh + 1)],
                                        start=False, stop=True)
                            return ins
                        vw2m = EMIT("t", _w2, "tT")
                        for dh in range(2):
                            bank_done("t", "pY%d" % (pp + dh), "tT", vw2m)
                        if j == NT - 1:
                            w2_free[ws] = ("tT", vw2m)
                            if q == 0:
                                b2_free[es] = ("tT", vw2m)
                        for dh in range(2):
                            bank_wait("v", "pY%d" % (pp + dh))
                        def _ga(e, e_i=e_i, j=j, pp=pp):
                            ins = None
                            for dh in range(2):
                                ins = e.scalar_tensor_tensor(
                                    acc[:, j * D + 512 * dh: j * D + 512 * (dh + 1)],
                                    pY[pp + dh][:, :],
                                    gates[:, j * E + e_i: j * E + e_i + 1],
                                    acc[:, j * D + 512 * dh: j * D + 512 * (dh + 1)],
                                    op0=ALU.mult, op1=ALU.add)
                            return ins
                        vga = EMIT("v", _ga, "tV")
                        for dh in range(2):
                            bank_done("v", "pY%d" % (pp + dh), "tV", vga)

            for ek in ["t", "s", "v"]:
                W(ek, OWN[ek], P.sem_val[OWN[ek]])
                P.add(ek, lambda e: e.nop().then_inc(sem["ph"], 1))
                P.inc("ph")
            W("y", "dA", P.sem_val["dA"])
            W("y", "dB0", 16 * b1_loads[0])
            W("y", "dB1", 16 * b1_loads[1])
            P.add("y", lambda e: e.nop().then_inc(sem["ph"], 1))
            P.inc("ph")
            for sn in ["dW1a", "dW1b", "dW2a", "dW2b", "dB2a", "dB2b"]:
                W("g", sn, P.sem_val.get(sn, 0))
            P.add("g", lambda e: e.nop().then_inc(sem["ph"], 1))
            bar_f = P.inc("ph")
        for ek in ["t", "s", "v", "g", "y"]:
            W(ek, "ph", bar_f)

        with contextlib.ExitStack() as fo:
            x_sb2 = fo.enter_context(nc.sbuf_tensor("x_sb2", [128, 2 * D], F32))
            slot_free = [None, None]
            DR("v")
            for j in range(NT):
                slot = j % 2
                dxs = "dX0" if slot == 0 else "dX1"
                if slot_free[slot] is not None:
                    W("y", slot_free[slot][0], slot_free[slot][1])
                vload = EMIT("y", lambda e, j=j, slot=slot: e.dma_start(
                    out=x_sb2[:, slot * D:(slot + 1) * D],
                    in_=x_h.rearrange("(j p) d -> j p d", p=128)[j]), dxs, 16)
                W("v", dxs, vload)
                vadd = EMIT("v", lambda e, j=j, slot=slot: e.tensor_add(
                    acc[:, j * D:(j + 1) * D],
                    acc[:, j * D:(j + 1) * D],
                    x_sb2[:, slot * D:(slot + 1) * D]), "tV")
                DR("v")
                slot_free[slot] = ("tV", vadd)
                W("y", "tV", vadd)
                EMIT("y", lambda e, j=j: e.dma_start(
                    out=out_h.rearrange("(j p) d -> j p d", p=128)[j],
                    in_=acc[:, j * D:(j + 1) * D]), "dOut", 16)
            W("y", "dOut", P.sem_val["dOut"])

        @block.tensor
        def _(eng):
            for fn in P.steps["t"]:
                fn(eng)

        @block.scalar
        def _(eng):
            for fn in P.steps["s"]:
                fn(eng)

        @block.vector
        def _(eng):
            for fn in P.steps["v"]:
                fn(eng)

        @block.gpsimd
        def _(eng):
            for fn in P.steps["g"]:
                fn(eng)

        @block.sync
        def _(eng):
            for fn in P.steps["y"]:
                fn(eng)

    return nc


def kernel(L0_attn_x, L0_attn_wq, L0_attn_wk, L0_attn_wv, L0_attn_wo,
           L0_ffn_Wr, L0_ffn_br, L0_ffn_W1, L0_ffn_b1, L0_ffn_W2, L0_ffn_b2):
    x = np.ascontiguousarray(np.asarray(L0_attn_x, dtype=np.float32))
    B = x.shape[0]
    shared = dict(
        wq=np.ascontiguousarray(np.asarray(L0_attn_wq, np.float32)),
        wk=np.ascontiguousarray(np.asarray(L0_attn_wk, np.float32)),
        wv=np.ascontiguousarray(np.asarray(L0_attn_wv, np.float32)),
        wo=np.ascontiguousarray(np.asarray(L0_attn_wo, np.float32)),
        wr=np.ascontiguousarray(np.asarray(L0_ffn_Wr, np.float32)),
        br=np.ascontiguousarray(np.asarray(L0_ffn_br, np.float32)).reshape(1, E),
        w1=np.ascontiguousarray(np.asarray(L0_ffn_W1, np.float32)),
        b1=np.ascontiguousarray(np.asarray(L0_ffn_b1, np.float32)),
        w2=np.ascontiguousarray(np.asarray(L0_ffn_W2, np.float32)),
        b2=np.ascontiguousarray(np.asarray(L0_ffn_b2, np.float32)),
        ident=np.eye(128, dtype=np.float32),
        tril=np.triu(np.ones((128, 128), np.float32)),
    )
    nc = build_nc()
    in_maps = [dict(x=x[c], **shared) for c in range(B)]
    res = run_bass_kernel_spmd(nc, in_maps, core_ids=list(range(B)))
    out = np.stack([res.results[c]["out"] for c in range(B)], axis=0)
    return out


# version 04
# speedup vs baseline: 1.2723x; 1.0819x over previous
import contextlib
import numpy as np
import concourse.bass as bass
import concourse.mybir as mybir
from concourse.bass_utils import run_bass_kernel_spmd

F32 = mybir.dt.float32
BF16 = mybir.dt.bfloat16
AF = mybir.ActivationFunctionType
ALU = mybir.AluOpType
AX = mybir.AxisListType

S = 2048
D = 1024
NH = 16
E = 8
NT = S // 128
ND = D // 128
NQC = S // 512


class Plan:
    def __init__(self):
        self.steps = {"t": [], "s": [], "v": [], "g": [], "y": []}
        self.sem_val = {}

    def add(self, eng, fn):
        self.steps[eng].append(fn)

    def inc(self, sem_name, amount=1):
        v = self.sem_val.get(sem_name, 0) + amount
        self.sem_val[sem_name] = v
        return v


def build_nc():
    nc = bass.Bass("TRN2", target_bir_lowering=False)

    x_h = nc.dram_tensor("x", [S, D], F32, kind="ExternalInput").ap()
    wq_h = nc.dram_tensor("wq", [D, D], F32, kind="ExternalInput").ap()
    wk_h = nc.dram_tensor("wk", [D, D], F32, kind="ExternalInput").ap()
    wv_h = nc.dram_tensor("wv", [D, D], F32, kind="ExternalInput").ap()
    wo_h = nc.dram_tensor("wo", [D, D], F32, kind="ExternalInput").ap()
    wr_h = nc.dram_tensor("wr", [D, E], F32, kind="ExternalInput").ap()
    br_h = nc.dram_tensor("br", [1, E], F32, kind="ExternalInput").ap()
    w1_h = nc.dram_tensor("w1", [E, D, 4096], F32, kind="ExternalInput").ap()
    b1_h = nc.dram_tensor("b1", [E, 4096], F32, kind="ExternalInput").ap()
    w2_h = nc.dram_tensor("w2", [E, 4096, D], F32, kind="ExternalInput").ap()
    b2_h = nc.dram_tensor("b2", [E, D], F32, kind="ExternalInput").ap()
    id_h = nc.dram_tensor("ident", [128, 128], F32, kind="ExternalInput").ap()
    tril_h = nc.dram_tensor("tril", [128, 128], F32, kind="ExternalInput").ap()
    out_h = nc.dram_tensor("out", [S, D], F32, kind="ExternalOutput").ap()
    xnts_h = nc.dram_tensor("xnts", [128, ND * S], BF16, kind="Internal").ap()

    P = Plan()
    SEMS = ["dX0", "dX1", "dA", "dC", "dWq", "dWo", "dW1a", "dW1b",
            "dW2a", "dW2b", "dB0", "dB1", "dB2a", "dB2b", "dOut",
            "tT", "tS", "tV", "ph"]
    OWN = {"t": "tT", "s": "tS", "v": "tV"}

    with contextlib.ExitStack() as ctx:
        sem = {n: ctx.enter_context(nc.semaphore(n)) for n in SEMS}

        def W(eng_key, sname, val):
            if val > 0:
                P.add(eng_key, lambda e, s=sem[sname], v=val: e.wait_ge(s, v))

        def DR(eng_key):
            P.add(eng_key, lambda e: e.drain())

        def EMIT(eng_key, fn, inc_sem=None, inc_by=1):
            if inc_sem is None:
                P.add(eng_key, fn)
                return None
            v = P.inc(inc_sem, inc_by)
            P.add(eng_key, lambda e, f=fn, s=sem[inc_sem], a=inc_by: f(e).then_inc(s, a))
            return v

        bank_last = {}

        def bank_wait(eng_key, bank):
            la = bank_last.get(bank)
            if la is not None and la[0] != eng_key:
                W(eng_key, la[1], la[2])

        def bank_done(eng_key, bank, sname, val):
            bank_last[bank] = (eng_key, sname, val)

        gates = ctx.enter_context(nc.sbuf_tensor("gates", [128, NT * E], F32))
        id_f = ctx.enter_context(nc.sbuf_tensor("id_f", [128, 128], F32))
        id_b = ctx.enter_context(nc.sbuf_tensor("id_b", [128, 128], BF16))
        tril_b = ctx.enter_context(nc.sbuf_tensor("tril_b", [128, 128], BF16))
        br_t = ctx.enter_context(nc.sbuf_tensor("br_t", [128, E], F32))
        ones_b = ctx.enter_context(nc.sbuf_tensor("ones_b", [1, 128], BF16))
        eps_t = ctx.enter_context(nc.sbuf_tensor("eps_t", [128, 1], F32))
        lg_all = ctx.enter_context(nc.sbuf_tensor("lg_all", [128, NT * E], F32))
        rt8 = ctx.enter_context(nc.sbuf_tensor("rt8", [128, 6 * E], F32))
        rs1 = ctx.enter_context(nc.sbuf_tensor("rs1", [128, 8], F32))
        wr_sb = ctx.enter_context(nc.sbuf_tensor("wr_sb", [128, ND * E], BF16))
        qkv_scope = ctx.enter_context(contextlib.ExitStack())
        qT = qkv_scope.enter_context(nc.sbuf_tensor("qT", [128, ND * S], BF16))
        kT = qkv_scope.enter_context(nc.sbuf_tensor("kT", [128, ND * S], BF16))
        v_sb = qkv_scope.enter_context(nc.sbuf_tensor("v_sb", [128, NT * NH * 65], BF16))
        acc_scope = ctx.enter_context(contextlib.ExitStack())
        acc = None

        block = ctx.enter_context(nc.Block())

        EMIT("y", lambda e: e.dma_start(out=id_f[:, :], in_=id_h), "dA", 16)
        EMIT("g", lambda e: e.dma_start(out=id_b[:, :], in_=id_h), "dC", 16)
        EMIT("g", lambda e: e.dma_start(out=tril_b[:, :], in_=tril_h), "dC", 16)
        EMIT("g", lambda e: e.dma_start(out=br_t[:, :], in_=br_h.partition_broadcast(128)), "dC", 16)
        EMIT("g", lambda e: e.dma_start(
            out=wr_sb[:, :].rearrange("p (t e) -> p t e", t=ND),
            in_=wr_h.rearrange("(t p) e -> p t e", p=128)), "dC", 16)
        vC = P.sem_val["dC"]
        EMIT("v", lambda e: e.memset(ones_b[:, :], 1.0), "tV")
        veps = EMIT("v", lambda e: e.memset(eps_t[:, :], 1e-5), "tV")
        EMIT("v", lambda e: e.memset(
            v_sb[:, :].rearrange("p (g c) -> p g c", c=65)[:, :, 64:65], 1.0), "tV")

        with contextlib.ExitStack() as a1:
            xnT = a1.enter_context(nc.sbuf_tensor("xnT", [128, ND * S], BF16))
            x_sb = a1.enter_context(nc.sbuf_tensor("x_sb", [128, 2 * D], F32))
            xnb = a1.enter_context(nc.sbuf_tensor("xnb", [128, D], BF16))
            bs_t = a1.enter_context(nc.sbuf_tensor("bs_t", [128, 2 * 12], F32))
            ba_t = a1.enter_context(nc.sbuf_tensor("ba_t", [128, 2 * 2], F32))
            st1 = a1.enter_context(nc.sbuf_tensor("st1", [128, 8], F32))
            w_sb = a1.enter_context(nc.sbuf_tensor("w_sb", [128, ND * D], BF16))

            pbT = a1.enter_context(nc.psum_tensor("pbT", [128, 1024], BF16))
            pb0 = a1.enter_context(nc.psum_tensor("pb0", [128, 512], F32))
            pb1 = a1.enter_context(nc.psum_tensor("pb1", [128, 512], F32))
            pb2 = a1.enter_context(nc.psum_tensor("pb2", [128, 512], F32))

            EMIT("g", lambda e: e.dma_start(
                out=w_sb[:, :].rearrange("p (t f) -> p t f", t=ND),
                in_=wq_h.rearrange("(t p) f -> p t f", p=128)), "dWq", 16)

            x_slot_free = [None, None]
            last_xn_evac = None
            last_tr = None

            for j in range(NT):
                slot = j % 2
                dxs = "dX0" if slot == 0 else "dX1"
                if x_slot_free[slot] is not None:
                    W("y", x_slot_free[slot][0], x_slot_free[slot][1])
                vload = EMIT("y", lambda e, j=j, slot=slot: e.dma_start(
                    out=x_sb[:, slot * D:(slot + 1) * D],
                    in_=x_h.rearrange("(j p) d -> j p d", p=128)[j]), dxs, 16)
                W("v", dxs, vload)
                so = slot * 12
                sa = slot * 2
                EMIT("v", lambda e, slot=slot, so=so: e.bn_stats(
                    bs_t[:, so:so + 6], x_sb[:, slot * D: slot * D + 512]))
                EMIT("v", lambda e, slot=slot, so=so: e.bn_stats(
                    bs_t[:, so + 6:so + 12], x_sb[:, slot * D + 512:(slot + 1) * D]))
                DR("v")
                vvar = EMIT("v", lambda e, so=so, sa=sa: e.bn_aggr(
                    ba_t[:, sa:sa + 2], bs_t[:, so:so + 12]), "tV")
                W("s", "tV", vvar)
                EMIT("s", lambda e, sa=sa, slot=slot: e.activation(
                    st1[:, 2 + slot:3 + slot], ba_t[:, sa + 1:sa + 2], AF.Ln,
                    bias=eps_t[:, 0:1], scale=1.0))
                DR("s")
                vrstd = EMIT("s", lambda e, slot=slot: e.activation(
                    st1[:, 4 + slot:5 + slot], st1[:, 2 + slot:3 + slot], AF.Exp,
                    scale=-0.5), "tS")
                W("v", "tS", vrstd)
                if last_tr is not None:
                    W("v", "tT", last_tr)
                DR("v")
                vxn = EMIT("v", lambda e, slot=slot, sa=sa: e.tensor_scalar(
                    xnb[:, :], x_sb[:, slot * D:(slot + 1) * D],
                    ba_t[:, sa:sa + 1], st1[:, 4 + slot:5 + slot],
                    op0=ALU.subtract, op1=ALU.mult), "tV")
                x_slot_free[slot] = ("tV", vxn)
                W("t", "tV", vxn)
                W("t", "dC", vC)
                bank_wait("t", "pbT")
                def _tr(e, j=j):
                    ins = None
                    for d in range(ND):
                        ins = e.transpose(pbT[:, d * 128:(d + 1) * 128],
                                          xnb[:, d * 128:(d + 1) * 128], id_b[:, :])
                    return ins
                vtr = EMIT("t", _tr, "tT")
                last_tr = vtr
                bank_done("t", "pbT", "tT", vtr)
                bank_wait("v", "pbT")
                def _ev(e, j=j):
                    ins = None
                    for d in range(ND):
                        ins = e.tensor_copy(xnT[:, d * S + 128 * j: d * S + 128 * (j + 1)],
                                            pbT[:, d * 128:(d + 1) * 128])
                    return ins
                vev = EMIT("v", _ev, "tV")
                bank_done("v", "pbT", "tV", vev)
                last_xn_evac = vev

                W("t", "tV", vev)
                bank_wait("t", "pb2")
                def _rt(e, j=j):
                    ins = None
                    for d in range(ND):
                        ins = e.matmul(pb2[:, 0:E],
                                       xnT[:, d * S + 128 * j: d * S + 128 * (j + 1)],
                                       wr_sb[:, d * E:(d + 1) * E],
                                       start=(d == 0), stop=(d == ND - 1))
                    return ins
                vrt = EMIT("t", _rt, "tT")
                bank_done("t", "pb2", "tT", vrt)
                bank_wait("v", "pb2")
                vpb2 = EMIT("v", lambda e, j=j: e.tensor_add(
                    lg_all[:, j * E:(j + 1) * E], pb2[:, 0:E], br_t[:, :]), "tV")
                bank_done("v", "pb2", "tV", vpb2)
                DR("v")

            def proj_qk(dst, dsem_val):
                for u in range(ND):
                    for scq in range(NQC):
                        par = (u * NQC + scq) % 2
                        pb = pb0 if par == 0 else pb1
                        pbn = "pb0" if par == 0 else "pb1"
                        W("t", "dWq", dsem_val)
                        if u == 0 and scq == 0:
                            W("t", "tV", last_xn_evac)
                        bank_wait("t", pbn)
                        def _mm(e, u=u, scq=scq, pb=pb):
                            ins = None
                            for d in range(ND):
                                ins = e.matmul(pb[:, :],
                                               w_sb[:, d * D + 128 * u: d * D + 128 * (u + 1)],
                                               xnT[:, d * S + 512 * scq: d * S + 512 * (scq + 1)],
                                               start=(d == 0), stop=(d == ND - 1))
                            return ins
                        vmm = EMIT("t", _mm, "tT")
                        bank_done("t", pbn, "tT", vmm)
                        ekey = "s" if par == 0 else "v"
                        bank_wait(ekey, pbn)
                        if ekey == "s":
                            vev = EMIT("s", lambda e, u=u, scq=scq, pb=pb, dst=dst: e.activation(
                                dst[:, u * S + 512 * scq: u * S + 512 * (scq + 1)],
                                pb[:, :], AF.Copy), "tS")
                        else:
                            vev = EMIT("v", lambda e, u=u, scq=scq, pb=pb, dst=dst: e.tensor_copy(
                                dst[:, u * S + 512 * scq: u * S + 512 * (scq + 1)],
                                pb[:, :]), "tV")
                        bank_done(ekey, pbn, OWN[ekey], vev)

            proj_qk(qT, 16)
            W("g", "tT", P.sem_val["tT"])
            vwk = EMIT("g", lambda e: e.dma_start(
                out=w_sb[:, :].rearrange("p (t f) -> p t f", t=ND),
                in_=wk_h.rearrange("(t p) f -> p t f", p=128)), "dWq", 16)
            proj_qk(kT, vwk)
            W("g", "tT", P.sem_val["tT"])
            vwv = EMIT("g", lambda e: e.dma_start(
                out=w_sb[:, :].rearrange("p (t f) -> p t f", t=ND),
                in_=wv_h.rearrange("(t p) f -> p t f", p=128)), "dWq", 16)
            for j in range(NT):
                for ch in range(2):
                    par = (j * 2 + ch) % 2
                    pb = pb0 if par == 0 else pb1
                    pbn = "pb0" if par == 0 else "pb1"
                    W("t", "dWq", vwv)
                    bank_wait("t", pbn)
                    def _mmv(e, j=j, ch=ch, pb=pb):
                        ins = None
                        for d in range(ND):
                            ins = e.matmul(pb[:, :],
                                           xnT[:, d * S + 128 * j: d * S + 128 * (j + 1)],
                                           w_sb[:, d * D + 512 * ch: d * D + 512 * (ch + 1)],
                                           start=(d == 0), stop=(d == ND - 1))
                        return ins
                    vmm = EMIT("t", _mmv, "tT")
                    bank_done("t", pbn, "tT", vmm)
                    ekey = "s" if ch == 0 else "v"
                    bank_wait(ekey, pbn)
                    base = j * NH * 65 + ch * 8 * 65
                    if ekey == "s":
                        vev = EMIT("s", lambda e, base=base, pb=pb: e.activation(
                            v_sb[:, base:base + 8 * 65].rearrange("p (h c) -> p h c", c=65)[:, :, 0:64],
                            pb[:, :].rearrange("p (h c) -> p h c", c=64), AF.Copy), "tS")
                    else:
                        vev = EMIT("v", lambda e, base=base, pb=pb: e.tensor_copy(
                            v_sb[:, base:base + 8 * 65].rearrange("p (h c) -> p h c", c=65)[:, :, 0:64],
                            pb[:, :].rearrange("p (h c) -> p h c", c=64)), "tV")
                    bank_done(ekey, pbn, OWN[ekey], vev)

            W("y", "tV", P.sem_val["tV"])
            W("y", "tS", P.sem_val["tS"])
            EMIT("y", lambda e: e.dma_start(out=xnts_h, in_=xnT[:, :]), "dA", 16)

            for ek in ["t", "s", "v"]:
                W(ek, OWN[ek], P.sem_val[OWN[ek]])
                P.add(ek, lambda e: e.nop().then_inc(sem["ph"], 1))
                P.inc("ph")
            W("y", "dA", P.sem_val["dA"])
            W("y", "dX0", P.sem_val["dX0"])
            W("y", "dX1", P.sem_val["dX1"])
            P.add("y", lambda e: e.nop().then_inc(sem["ph"], 1))
            P.inc("ph")
            W("g", "dWq", P.sem_val["dWq"])
            W("g", "dC", P.sem_val["dC"])
            P.add("g", lambda e: e.nop().then_inc(sem["ph"], 1))
            bar_a1 = P.inc("ph")
        for ek in ["t", "s", "v", "g", "y"]:
            W(ek, "ph", bar_a1)
        bank_last.clear()
        acc = acc_scope.enter_context(nc.sbuf_tensor("acc", [128, NT * D], F32, side="right"))

        with contextlib.ExitStack() as a2:
            pT = a2.enter_context(nc.sbuf_tensor("pT", [128, 16 * 512], BF16))
            ctx_all = a2.enter_context(nc.sbuf_tensor("ctx_all", [128, 8 * 512], BF16))
            wo_sb = a2.enter_context(nc.sbuf_tensor("wo_sb", [128, 8 * 1024], BF16))
            rrow_f = a2.enter_context(nc.sbuf_tensor("rrow_f", [1, 512], F32))
            rrow_b = a2.enter_context(nc.sbuf_tensor("rrow_b", [1, 512], BF16))
            bc_sb = a2.enter_context(nc.sbuf_tensor("bc_sb", [64, 512], BF16))

            pL = [a2.enter_context(nc.psum_tensor("pL%d" % i, [128, 512], F32))
                  for i in range(4)]
            pC = [a2.enter_context(nc.psum_tensor("pC0", [65, 512], F32)),
                  a2.enter_context(nc.psum_tensor("pC1", [65, 512], F32))]
            pB = a2.enter_context(nc.psum_tensor("pB", [64, 512], F32))
            pO = [pL[0], pL[1]]

            vwo = EMIT("g", lambda e: e.dma_start(
                out=wo_sb[:, :].rearrange("p (r d) -> p r d", r=8),
                in_=wo_h.rearrange("(r p) d -> p r d", p=128)), "dWo", 16)

            lctr = 0
            pv_reads = {}
            gate_vals = {}
            wo_waited = False

            def emit_gates(j):
                lg = lg_all[:, j * E:(j + 1) * E]
                eg = rt8[:, 1 * E:2 * E]
                ind = rt8[:, 2 * E:3 * E]
                ind2 = rt8[:, 3 * E:4 * E]
                u = rt8[:, 4 * E:5 * E]
                p_ = rt8[:, 5 * E:6 * E]
                DR("v")
                EMIT("v", lambda e, lg=lg: e.reduce_max(rs1[:, 0:1], lg, axis=AX.X, negate=True))
                DR("v")
                EMIT("v", lambda e, lg=lg, ind=ind: e.tensor_scalar(
                    ind, lg, rs1[:, 0:1], 0.0, op0=ALU.add, op1=ALU.is_ge))
                DR("v")
                EMIT("v", lambda e, lg=lg, ind=ind, u=u: e.scalar_tensor_tensor(
                    u, ind, -1e30, lg, op0=ALU.mult, op1=ALU.add))
                DR("v")
                vnm2 = EMIT("v", lambda e, u=u: e.reduce_max(
                    rs1[:, 1:2], u, axis=AX.X, negate=True), "tV")
                W("s", "tV", vnm2)
                DR("s")
                vexp = EMIT("s", lambda e, lg=lg, eg=eg: e.activation(
                    eg, lg, AF.Exp, bias=rs1[:, 0:1], scale=1.0), "tS")
                W("v", "tS", vexp)
                EMIT("v", lambda e, lg=lg, ind2=ind2: e.tensor_scalar(
                    ind2, lg, rs1[:, 1:2], 0.0, op0=ALU.add, op1=ALU.is_ge))
                DR("v")
                EMIT("v", lambda e, eg=eg, ind2=ind2, p_=p_: e.tensor_mul(p_, eg, ind2))
                DR("v")
                EMIT("v", lambda e, p_=p_: e.reduce_sum(rs1[:, 2:3], p_, axis=AX.X))
                DR("v")
                EMIT("v", lambda e: e.reciprocal(rs1[:, 3:4], rs1[:, 2:3]))
                DR("v")
                EMIT("v", lambda e, j=j, p_=p_: e.tensor_scalar_mul(
                    gates[:, j * E:(j + 1) * E], p_, rs1[:, 3:4]), "tV")
                DR("v")

            DEPTH = 3
            pending_norm = [None]
            last_vbc = [None]
            for qc in range(NQC):
                ctx_ready = []
                for h in range(NH):
                    hp = h % 2
                    hb = h // 2
                    cpn = "pC%d" % hp
                    nt_ = 4 * qc + 4
                    soff = 8 * hp if nt_ <= 8 else 0
                    for i in range(nt_ + DEPTH):
                        if i < nt_:
                            t = i
                            sl = soff + t
                            qv0 = max(0, 128 * t - 512 * qc)
                            lb = lctr % 4
                            lctr += 1
                            lbn = "pL%d" % lb
                            bank_wait("t", lbn)
                            vlg = EMIT("t", lambda e, t=t, qc=qc, hp=hp, hb=hb, qv0=qv0, lb=lb:
                                       e.matmul(pL[lb][:, qv0:512],
                                                kT[64 * hp:64 * hp + 64,
                                                   hb * S + 128 * t: hb * S + 128 * (t + 1)],
                                                qT[64 * hp:64 * hp + 64,
                                                   hb * S + 512 * qc + qv0: hb * S + 512 * (qc + 1)],
                                                start=True, stop=True), "tT")
                            bank_done("t", lbn, "tT", vlg)
                            bank_wait("s", lbn)
                            if sl in pv_reads:
                                W("s", "tT", pv_reads[sl])
                            vex = EMIT("s", lambda e, sl=sl, qv0=qv0, lb=lb: e.activation(
                                pT[:, 512 * sl + qv0: 512 * (sl + 1)],
                                pL[lb][:, qv0:512], AF.Exp, scale=0.125), "tS")
                            bank_done("s", lbn, "tS", vex)
                            gate_vals[t] = ("tS", vex)
                            if t >= 4 * qc:
                                W("v", "tS", vex)
                                vmask = EMIT("v", lambda e, sl=sl, qv0=qv0: e.tensor_mul(
                                    pT[:, 512 * sl + qv0: 512 * sl + qv0 + 128],
                                    pT[:, 512 * sl + qv0: 512 * sl + qv0 + 128],
                                    tril_b[:, :]), "tV")
                                DR("v")
                                gate_vals[t] = ("tV", vmask)
                        if i == 1 and pending_norm[0] is not None:
                            pending_norm[0]()
                            pending_norm[0] = None
                        if i >= DEPTH:
                            t = i - DEPTH
                            sl = soff + t
                            qv0 = max(0, 128 * t - 512 * qc)
                            gv = gate_vals[t]
                            W("t", gv[0], gv[1])
                            if t == 0:
                                bank_wait("t", cpn)
                            vpv = EMIT("t", lambda e, t=t, sl=sl, h=h, hp=hp, qv0=qv0,
                                       first=(t == 0), last=(t == nt_ - 1):
                                       e.matmul(pC[hp][:, qv0:512],
                                                v_sb[:, t * NH * 65 + 65 * h: t * NH * 65 + 65 * h + 65],
                                                pT[:, 512 * sl + qv0: 512 * (sl + 1)],
                                                start=first, stop=last), "tT")
                            pv_reads[sl] = vpv
                    bank_done("t", cpn, "tT", vpv)
                    bank_wait("s", cpn)
                    if last_vbc[0] is not None:
                        W("s", "tT", last_vbc[0])
                    DR("s")
                    EMIT("s", lambda e, hp=hp: e.activation(rrow_f[:, :], pC[hp][64:65, :], AF.Ln))
                    DR("s")
                    vrb = EMIT("s", lambda e: e.activation(rrow_b[:, :], rrow_f[:, :], AF.Exp,
                                                           scale=-1.0), "tS")
                    bank_done("s", cpn, "tS", vrb)

                    def _norm_tail(hp=hp, hb=hb, cpn=cpn, vrb=vrb):
                        W("t", "tS", vrb)
                        bank_wait("t", "pB")
                        vbc = EMIT("t", lambda e: e.matmul(pB[0:64, :], ones_b[:, 0:64], rrow_b[:, :],
                                                           start=True, stop=True), "tT")
                        last_vbc[0] = vbc
                        bank_done("t", "pB", "tT", vbc)
                        bank_wait("v", "pB")
                        EMIT("v", lambda e: e.tensor_copy(bc_sb[:, :], pB[0:64, :]))
                        DR("v")
                        vnz = EMIT("v", lambda e, hp=hp, hb=hb: e.tensor_mul(
                            ctx_all[64 * hp:64 * hp + 64, 512 * hb: 512 * (hb + 1)],
                            pC[hp][0:64, :], bc_sb[:, :]), "tV")
                        DR("v")
                        bank_done("v", "pB", "tV", vnz)
                        bank_done("v", cpn, "tV", vnz)
                        ctx_ready.append(vnz)
                    pending_norm[0] = _norm_tail
                    if qc == 0:
                        emit_gates(h)
                if pending_norm[0] is not None:
                    pending_norm[0]()
                    pending_norm[0] = None
                if not wo_waited:
                    W("t", "dWo", vwo)
                    wo_waited = True
                for j4 in range(4):
                    for half in range(2):
                        bank_wait("t", "pL0")
                        bank_wait("t", "pL1")
                        W("t", "tV", ctx_ready[-1])
                        def _op(e, j4=j4, half=half):
                            ins = None
                            for h in range(NH):
                                hp = h % 2
                                hb = h // 2
                                ins = e.matmul(
                                    pO[hp][:, :],
                                    ctx_all[64 * hp:64 * hp + 64,
                                            512 * hb + 128 * j4: 512 * hb + 128 * (j4 + 1)],
                                    wo_sb[64 * hp:64 * hp + 64,
                                          1024 * hb + 512 * half: 1024 * hb + 512 * (half + 1)],
                                    start=(h < 2), stop=(h >= NH - 2))
                            return ins
                        vop = EMIT("t", _op, "tT")
                        bank_done("t", "pL0", "tT", vop)
                        bank_done("t", "pL1", "tT", vop)
                        bank_wait("v", "pL0")
                        bank_wait("v", "pL1")
                        jg = 4 * qc + j4
                        EMIT("v", lambda e, jg=jg, half=half: e.tensor_copy(
                            acc[:, jg * D + 512 * half: jg * D + 512 * (half + 1)], pO[0][:, :]))
                        DR("v")
                        voe = EMIT("v", lambda e, jg=jg, half=half: e.tensor_add(
                            acc[:, jg * D + 512 * half: jg * D + 512 * (half + 1)],
                            acc[:, jg * D + 512 * half: jg * D + 512 * (half + 1)],
                            pO[1][:, :]), "tV")
                        DR("v")
                        bank_done("v", "pL0", "tV", voe)
                        bank_done("v", "pL1", "tV", voe)

            for ek in ["t", "s", "v"]:
                W(ek, OWN[ek], P.sem_val[OWN[ek]])
                P.add(ek, lambda e: e.nop().then_inc(sem["ph"], 1))
                P.inc("ph")
            W("y", "dA", P.sem_val["dA"])
            P.add("y", lambda e: e.nop().then_inc(sem["ph"], 1))
            P.inc("ph")
            W("g", "dWo", P.sem_val["dWo"])
            P.add("g", lambda e: e.nop().then_inc(sem["ph"], 1))
            bar_a2 = P.inc("ph")
        for ek in ["t", "s", "v", "g", "y"]:
            W(ek, "ph", bar_a2)
        bank_last.clear()
        qkv_scope.close()

        with contextlib.ExitStack() as f:
            xnT2 = f.enter_context(nc.sbuf_tensor("xnT2", [128, ND * S], BF16))
            hT = f.enter_context(nc.sbuf_tensor("hT", [128, 8 * S], BF16))
            w2q = f.enter_context(nc.sbuf_tensor("w2q", [128, 2 * 8 * D], BF16))
            w1r = f.enter_context(nc.sbuf_tensor("w1r", [128, 2 * ND * 512], BF16))
            b1st = f.enter_context(nc.sbuf_tensor("b1st", [32, 2 * 128], F32))
            b1sb = f.enter_context(nc.sbuf_tensor("b1sb", [128, 2 * 32], F32))
            b2r = f.enter_context(nc.sbuf_tensor("b2r", [1, 2 * D], BF16))

            pH = [f.enter_context(nc.psum_tensor("pH%d" % i, [128, 512], F32)) for i in range(4)]
            pY = [f.enter_context(nc.psum_tensor("pY%d" % i, [128, 512], F32)) for i in range(4)]

            vxl = EMIT("y", lambda e: e.dma_start(out=xnT2[:, :], in_=xnts_h), "dA", 16)
            W("t", "dA", vxl)
            DR("v")

            w1_free = [None, None]
            w2_free = [None, None]
            b1_free = [None, None]
            b2_free = [None, None]
            w1_loads = [0, 0]
            w2_loads = [0, 0]
            b1_loads = [0, 0]
            b2_loads = [0, 0]
            w1_idx = 0

            for e_i in range(E):
                es = e_i % 2
                if b1_free[es] is not None:
                    W("y", b1_free[es][0], b1_free[es][1])
                b1_loads[es] += 1
                EMIT("y", lambda e, e_i=e_i, es=es: e.dma_start(
                    out=b1st[:, es * 128:(es + 1) * 128],
                    in_=b1_h[e_i].rearrange("(a b) -> a b", b=128)), "dB%d" % es, 16)
                if b2_free[es] is not None:
                    W("g", b2_free[es][0], b2_free[es][1])
                b2_loads[es] += 1
                EMIT("g", lambda e, e_i=e_i, es=es: e.dma_start(
                    out=b2r[:, es * D:(es + 1) * D],
                    in_=b2_h[e_i:e_i + 1, :]), "dB2%s" % ("a" if es == 0 else "b"), 16)
                W("t", "dB%d" % es, 16 * b1_loads[es])
                bank_wait("t", "pH0")
                vtr = EMIT("t", lambda e, es=es: e.transpose(
                    pH[0][:, 0:32], b1st[:, es * 128:(es + 1) * 128],
                    id_f[0:32, 0:32]), "tT")
                bank_done("t", "pH0", "tT", vtr)
                b1_free[es] = ("tT", vtr)
                bank_wait("v", "pH0")
                DR("v")
                vb1e = EMIT("v", lambda e, es=es: e.tensor_copy(
                    b1sb[:, es * 32:(es + 1) * 32], pH[0][:, 0:32]), "tV")
                DR("v")
                bank_done("v", "pH0", "tV", vb1e)
                b1_ready = vb1e

                for q in range(4):
                    ws = (e_i * 4 + q) % 2
                    if w2_free[ws] is not None:
                        W("g", w2_free[ws][0], w2_free[ws][1])
                    w2_loads[ws] += 1
                    EMIT("g", lambda e, e_i=e_i, q=q, ws=ws: e.dma_start(
                        out=w2q[:, ws * 8 * D:(ws + 1) * 8 * D].rearrange("p (t d) -> p t d", t=8),
                        in_=w2_h[e_i].rearrange("(qq t p) d -> qq p t d", qq=4, p=128)[q]),
                        "dW2%s" % ("a" if ws == 0 else "b"), 16)
                    w1v = []
                    for hh in range(2):
                        s_ = w1_idx % 2
                        w1_idx += 1
                        if w1_free[s_] is not None:
                            W("g", w1_free[s_][0], w1_free[s_][1])
                        w1_loads[s_] += 1
                        EMIT("g", lambda e, e_i=e_i, q=q, hh=hh, s_=s_: e.dma_start(
                            out=w1r[:, s_ * ND * 512:(s_ + 1) * ND * 512].rearrange(
                                "p (t f) -> p t f", t=ND),
                            in_=w1_h[e_i].rearrange("(t p) f -> p t f", p=128)[
                                :, :, 1024 * q + 512 * hh: 1024 * q + 512 * (hh + 1)]),
                            "dW1%s" % ("a" if s_ == 0 else "b"), 16)
                        w1v.append((s_, 16 * w1_loads[s_]))

                    hT_evacs = []
                    for ft in range(8):
                        hh = ft // 4
                        s_, wv_ = w1v[hh]
                        fo = (ft % 4) * 128
                        W("t", "dW1%s" % ("a" if s_ == 0 else "b"), wv_)
                        vw1m = None
                        for sc in range(NQC):
                            bank_wait("t", "pH%d" % sc)
                            def _w1(e, s_=s_, fo=fo, sc=sc):
                                ins = None
                                for d in range(ND):
                                    ins = e.matmul(
                                        pH[sc][:, :],
                                        w1r[:, s_ * ND * 512 + d * 512 + fo:
                                            s_ * ND * 512 + d * 512 + fo + 128],
                                        xnT2[:, d * S + 512 * sc: d * S + 512 * (sc + 1)],
                                        start=(d == 0), stop=(d == ND - 1))
                                return ins
                            vw1m = EMIT("t", _w1, "tT")
                            bank_done("t", "pH%d" % sc, "tT", vw1m)
                            ekey = "s" if sc < 2 else "v"
                            bank_wait(ekey, "pH%d" % sc)
                            if ekey == "s":
                                W("s", "tV", b1_ready)
                                vev = EMIT("s", lambda e, sc=sc, ft=ft, es=es, q=q: e.activation(
                                    hT[:, ft * S + 512 * sc: ft * S + 512 * (sc + 1)],
                                    pH[sc][:, :], AF.Relu,
                                    bias=b1sb[:, es * 32 + 8 * q + ft:
                                              es * 32 + 8 * q + ft + 1]), "tS")
                            else:
                                vev = EMIT("v", lambda e, sc=sc, ft=ft, es=es, q=q: e.tensor_scalar(
                                    hT[:, ft * S + 512 * sc: ft * S + 512 * (sc + 1)],
                                    pH[sc][:, :],
                                    b1sb[:, es * 32 + 8 * q + ft: es * 32 + 8 * q + ft + 1], 0.0,
                                    op0=ALU.add, op1=ALU.max), "tV")
                            bank_done(ekey, "pH%d" % sc, OWN[ekey], vev)
                            hT_evacs.append((ekey, vev))
                        if ft % 4 == 3:
                            w1_free[s_] = ("tT", vw1m)

                    W("t", "dW2%s" % ("a" if ws == 0 else "b"), 16 * w2_loads[ws])
                    if q == 0:
                        W("t", "dB2%s" % ("a" if es == 0 else "b"), 16 * b2_loads[es])
                    DR("v")
                    for j in range(NT):
                        pp = (j % 2) * 2
                        for dh in range(2):
                            bank_wait("t", "pY%d" % (pp + dh))
                        if j == 0:
                            for (ekey, vev) in hT_evacs:
                                W("t", OWN[ekey], vev)
                        def _w2(e, ws=ws, es=es, q=q, j=j, pp=pp):
                            ins = None
                            for ft in range(8):
                                for dh in range(2):
                                    ins = e.matmul(
                                        pY[pp + dh][:, :],
                                        hT[:, ft * S + 128 * j: ft * S + 128 * (j + 1)],
                                        w2q[:, ws * 8 * D + ft * D + 512 * dh:
                                            ws * 8 * D + ft * D + 512 * (dh + 1)],
                                        start=(ft == 0), stop=(ft == 7 and q != 0))
                            if q == 0:
                                for dh in range(2):
                                    ins = e.matmul(
                                        pY[pp + dh][:, :],
                                        ones_b[:, 0:128],
                                        b2r[:, es * D + 512 * dh: es * D + 512 * (dh + 1)],
                                        start=False, stop=True)
                            return ins
                        vw2m = EMIT("t", _w2, "tT")
                        for dh in range(2):
                            bank_done("t", "pY%d" % (pp + dh), "tT", vw2m)
                        if j == NT - 1:
                            w2_free[ws] = ("tT", vw2m)
                            if q == 0:
                                b2_free[es] = ("tT", vw2m)
                        for dh in range(2):
                            bank_wait("v", "pY%d" % (pp + dh))
                        def _ga(e, e_i=e_i, j=j, pp=pp):
                            ins = None
                            for dh in range(2):
                                ins = e.scalar_tensor_tensor(
                                    acc[:, j * D + 512 * dh: j * D + 512 * (dh + 1)],
                                    pY[pp + dh][:, :],
                                    gates[:, j * E + e_i: j * E + e_i + 1],
                                    acc[:, j * D + 512 * dh: j * D + 512 * (dh + 1)],
                                    op0=ALU.mult, op1=ALU.add)
                            return ins
                        vga = EMIT("v", _ga, "tV")
                        for dh in range(2):
                            bank_done("v", "pY%d" % (pp + dh), "tV", vga)

            for ek in ["t", "s", "v"]:
                W(ek, OWN[ek], P.sem_val[OWN[ek]])
                P.add(ek, lambda e: e.nop().then_inc(sem["ph"], 1))
                P.inc("ph")
            W("y", "dA", P.sem_val["dA"])
            W("y", "dB0", 16 * b1_loads[0])
            W("y", "dB1", 16 * b1_loads[1])
            P.add("y", lambda e: e.nop().then_inc(sem["ph"], 1))
            P.inc("ph")
            for sn in ["dW1a", "dW1b", "dW2a", "dW2b", "dB2a", "dB2b"]:
                W("g", sn, P.sem_val.get(sn, 0))
            P.add("g", lambda e: e.nop().then_inc(sem["ph"], 1))
            bar_f = P.inc("ph")
        for ek in ["t", "s", "v", "g", "y"]:
            W(ek, "ph", bar_f)

        with contextlib.ExitStack() as fo:
            x_sb2 = fo.enter_context(nc.sbuf_tensor("x_sb2", [128, 2 * D], F32))
            slot_free = [None, None]
            DR("v")
            for j in range(NT):
                slot = j % 2
                dxs = "dX0" if slot == 0 else "dX1"
                if slot_free[slot] is not None:
                    W("y", slot_free[slot][0], slot_free[slot][1])
                vload = EMIT("y", lambda e, j=j, slot=slot: e.dma_start(
                    out=x_sb2[:, slot * D:(slot + 1) * D],
                    in_=x_h.rearrange("(j p) d -> j p d", p=128)[j]), dxs, 16)
                W("v", dxs, vload)
                vadd = EMIT("v", lambda e, j=j, slot=slot: e.tensor_add(
                    acc[:, j * D:(j + 1) * D],
                    acc[:, j * D:(j + 1) * D],
                    x_sb2[:, slot * D:(slot + 1) * D]), "tV")
                DR("v")
                slot_free[slot] = ("tV", vadd)
                W("y", "tV", vadd)
                EMIT("y", lambda e, j=j: e.dma_start(
                    out=out_h.rearrange("(j p) d -> j p d", p=128)[j],
                    in_=acc[:, j * D:(j + 1) * D]), "dOut", 16)
            W("y", "dOut", P.sem_val["dOut"])

        @block.tensor
        def _(eng):
            for fn in P.steps["t"]:
                fn(eng)

        @block.scalar
        def _(eng):
            for fn in P.steps["s"]:
                fn(eng)

        @block.vector
        def _(eng):
            for fn in P.steps["v"]:
                fn(eng)

        @block.gpsimd
        def _(eng):
            for fn in P.steps["g"]:
                fn(eng)

        @block.sync
        def _(eng):
            for fn in P.steps["y"]:
                fn(eng)

    return nc


def kernel(L0_attn_x, L0_attn_wq, L0_attn_wk, L0_attn_wv, L0_attn_wo,
           L0_ffn_Wr, L0_ffn_br, L0_ffn_W1, L0_ffn_b1, L0_ffn_W2, L0_ffn_b2):
    x = np.ascontiguousarray(np.asarray(L0_attn_x, dtype=np.float32))
    B = x.shape[0]
    shared = dict(
        wq=np.ascontiguousarray(np.asarray(L0_attn_wq, np.float32)),
        wk=np.ascontiguousarray(np.asarray(L0_attn_wk, np.float32)),
        wv=np.ascontiguousarray(np.asarray(L0_attn_wv, np.float32)),
        wo=np.ascontiguousarray(np.asarray(L0_attn_wo, np.float32)),
        wr=np.ascontiguousarray(np.asarray(L0_ffn_Wr, np.float32)),
        br=np.ascontiguousarray(np.asarray(L0_ffn_br, np.float32)).reshape(1, E),
        w1=np.ascontiguousarray(np.asarray(L0_ffn_W1, np.float32)),
        b1=np.ascontiguousarray(np.asarray(L0_ffn_b1, np.float32)),
        w2=np.ascontiguousarray(np.asarray(L0_ffn_W2, np.float32)),
        b2=np.ascontiguousarray(np.asarray(L0_ffn_b2, np.float32)),
        ident=np.eye(128, dtype=np.float32),
        tril=np.triu(np.ones((128, 128), np.float32)),
    )
    nc = build_nc()
    in_maps = [dict(x=x[c], **shared) for c in range(B)]
    res = run_bass_kernel_spmd(nc, in_maps, core_ids=list(range(B)))
    out = np.stack([res.results[c]["out"] for c in range(B)], axis=0)
    return out
